# version 26
# baseline (speedup 1.0000x reference)
import math
import numpy as np
import concourse.bass as bass
import concourse.mybir as mybir
from concourse.bass_utils import run_bass_kernel_spmd

F32 = mybir.dt.float32
BF16 = mybir.dt.bfloat16
AF = mybir.ActivationFunctionType
ALU = mybir.AluOpType
AX = mybir.AxisListType.X

T = 2048
D = 1024
H = 16
HD = 64
CC = 8
TB = 16
NG = 4
NSEQ = 2
EPS = 1e-5

ENGS = ("sync", "tensor", "scalar", "vector", "gpsimd")
ENG_SEM = {"tensor": "pe", "scalar": "act", "vector": "dve", "gpsimd": "pool"}

DEBUG_TAGS = set()


class Prog:
    def __init__(self):
        self.q = {e: [] for e in ENGS}
        self.cnt = {s: 0 for s in ("in", "wld", "out", "pe", "act",
                                   "dve", "pool", "xi0", "xi1", "xi2", "xi3",
                                   "so0", "so1", "so2", "so3")}
        self.waited = {}
        self.sems = None
        self.a5g = None
        self.a7g = None

    def _wait(self, eng, tok, strict=False):
        if tok is None:
            return
        sem, v = tok
        if ENG_SEM.get(eng) == sem and not strict:
            return
        key = (eng, sem)
        if self.waited.get(key, 0) >= v:
            return
        self.waited[key] = v
        self.trace[eng].append(("wait", sem, v))
        self.q[eng].append(lambda e, sn=sem, vv=v: e.wait_ge(self.sems[sn], vv))

    trace = None

    def do(self, eng, fn, deps=(), inc=None, n=1, strict=False):
        if self.trace is None:
            self.trace = {e: [] for e in ENGS}
        for t in deps:
            self._wait(eng, t, strict=strict)
        if inc is None:
            self.trace[eng].append(("op", None, 0))
            self.q[eng].append(lambda e, f=fn: f(e))
            return None
        self.cnt[inc] += n
        v = self.cnt[inc]
        self.trace[eng].append(("op", inc, n))
        self.q[eng].append(
            lambda e, f=fn, sn=inc, nn=n: f(e).then_inc(self.sems[sn], nn))
        return (inc, v)


def host_constants():
    i = np.arange(0, HD, 2, dtype=np.float64)
    f = np.exp(i * (-math.log(10000.0) / HD))
    qq = np.arange(T, dtype=np.float64)
    sin = np.sin(qq[:, None] * f[None, :])
    cos = np.cos(qq[:, None] * f[None, :])
    U = np.zeros((T, HD))
    U[:, 0::2] = sin
    U[:, 1::2] = cos
    Ct = np.zeros((T, HD))
    Ct[:, 0::2] = cos
    Ct[:, 1::2] = cos
    St = np.zeros((T, HD))
    St[:, 0::2] = sin
    St[:, 1::2] = sin
    u2 = np.concatenate([U.T, U.T], axis=0).astype(np.float32)
    trig = np.concatenate([Ct.T, St.T], axis=0).astype(np.float32)
    dmask = np.zeros((128, 64), np.float32)
    for c in range(64):
        dmask[c, c] = 1.0
        dmask[64 + c, c] = 1.0
    sgn = np.ones((128, 1), np.float32)
    sgn[64:, 0] = np.tile([1.0, -1.0], 32)
    return u2, trig, dmask, sgn


class K:

    def __init__(self, nc, P, stack):
        self.nc, self.P = nc, P
        sb = lambda name, shape, dt: stack.enter_context(
            nc.sbuf_tensor(name, shape, dt))
        self.x_hbm = nc.dram_tensor("x", [NSEQ, D, T], F32,
                                    kind="ExternalInput").ap()
        self.out_hbm = nc.dram_tensor("out", [NSEQ, D, T], F32,
                                      kind="ExternalOutput").ap()
        dram_in = lambda name, shape: nc.dram_tensor(
            name, shape, F32, kind="ExternalInput").ap()
        self.wqkv = dram_in("wqkv", [24, 128, 1024])
        self.wout = dram_in("wout", [8, 128, 1024])
        self.wrt = dram_in("wrt", [8, 128, 1024])
        self.wkt = dram_in("wkt", [8, 128, 1024])
        self.wvt = dram_in("wvt", [8, 128, 1024])
        self.bqkv_h = dram_in("bqkv", [128, 24])
        self.bout_h = dram_in("bout", [128, 8])
        self.gln_h = dram_in("gln", [128, 8])
        self.bln_h = dram_in("bln", [128, 8])
        self.mur_h = dram_in("mur", [128, 8])
        self.muk_h = dram_in("muk", [128, 8])
        self.wrel2_h = dram_in("wrel2", [128, 16])
        self.sgn_h = dram_in("sgn", [128, 1])
        self.u2_h = dram_in("u2", [128, T])
        self.trig_h = dram_in("trig", [128, T])
        self.dmask_h = dram_in("dmask", [128, 64])

        self.x_sb = sb("x_sb", [128, CC, T], F32)
        self.a1 = sb("a1", [128, CC, T + 1], BF16)
        self.a2 = sb("a2", [128, CC, T], BF16)
        self.a4 = sb("a4", [128, 16640], BF16)
        self.a5 = sb("a5", [128, 1024], F32)
        self.a7 = sb("a7", [128, CC, 512], BF16)
        self.wv_g = sb("wv_g", [128, CC, 512], BF16)
        self.rows = sb("rows", [128, 2048], F32)
        self.mrep = sb("mrep", [128, T], BF16)
        self.mrep2 = sb("mrep2", [128, T], BF16)
        self.arep = sb("arep", [128, T], BF16)
        self.cs = sb("cs", [128, T], BF16)
        self.ones128 = sb("ones128", [128, 128], BF16)
        self.onesm = sb("onesm", [128, 1], BF16)
        self.nsel = sb("nsel", [128, 33], BF16)
        self.mh_t = sb("mh_t", [128, 128], BF16)
        self.grel = sb("grel", [128, 16], F32)
        self.dm = sb("dm", [128, 64], BF16)
        self.bb = sb("bb", [128, 16], F32)
        self.negm = sb("negm", [128, 4], F32)
        self.negmb = sb("negmb", [128, 4], BF16)
        self.negmb2 = sb("negmb2", [2, 1], F32)
        self.negval = sb("negval", [2, 128], BF16)
        self.onesw = sb("onesw", [2, 128], BF16)
        self.ident2 = sb("ident2", [2, 2], BF16)
        self.bbp = sb("bbp", [2, 8], F32)
        self.mvals = sb("mvals", [128, 8], F32)
        self.smf = sb("smf", [128, 4], F32)
        self.bqkv_sb = sb("bqkv_sb", [128, 24], F32)
        self.bqs = sb("bqs", [128, 8], F32)
        self.bout_sb = sb("bout_sb", [128, 8], F32)
        self.g_sb = sb("g_sb", [128, 8], F32)
        self.b_sb = sb("b_sb", [128, 8], F32)
        self.mur_sb = sb("mur_sb", [128, 8], F32)
        self.muk_sb = sb("muk_sb", [128, 8], F32)
        self.babs = sb("babs", [128, 16], BF16)
        self.cst = sb("cst", [128, 2], F32)
        nc.const_aps.register(F32, float(EPS), self.cst[:, 0:1])
        nc.const_aps.register(F32, float(EPS * EPS), self.cst[:, 1:2])
        self.ps = stack.enter_context(nc.psum_tensor("ps", [128, 4096], F32))

        self.rrow = self.rows[:, 1024:1536].bitcast(BF16)
        a4 = self.a4
        self.qs = [a4[:, 0:2048], a4[:, 2048:4096]]
        self.ks = [a4[:, 4096:6144], a4[:, 6144:8192]]
        self.vt = [a4[:, 8192:11392].rearrange("p (tb c) -> p tb c", tb=16),
                   a4[:, 11392:14592].rearrange("p (tb c) -> p tb c", tb=16)]
        self.pbuf = [a4[:, 14592:15616], a4[:, 15616:16640]]
        self.nsq = a4[:, 14592:16640]
        self.sc = [a4[:, 8192:10240], a4[:, 10240:12288]]
        self.ka_v = a4[:, 0:16384].rearrange("p (c t) -> p c t", c=CC)
        self.du_t = a4[:, 14592:16640]

        self.sems = {}
        for s in ("in", "wld", "out", "pe", "act", "dve", "pool",
                  "xi0", "xi1", "xi2", "xi3", "so0", "so1", "so2", "so3"):
            self.sems[s] = stack.enter_context(nc.semaphore("sem_" + s))
        P.sems = self.sems

    def dump(self, tag, ap, dep, dtype=None):
        if tag not in DEBUG_TAGS:
            return
        shape = list(ap.shape)
        dt = dtype or ap.tensor.dtype
        d = self.nc.dram_tensor("dbg_" + tag, shape, dt,
                                kind="ExternalOutput").ap()
        self.P.do("gpsimd", lambda e: e.dma_start(out=d, in_=ap),
                  deps=[dep], inc="out", n=16)

    def z(self, cc):
        return self.a1[:, cc, 1:T + 1]

    def zblk(self, cc, tb):
        return self.a1[:, cc, 1 + 128 * tb:1 + 128 * (tb + 1)]

    def load_slice(self, w_sliced, idx, dst, cast_deps=()):
        P = self.P
        td = P.do("sync", lambda e, w=w_sliced, i=idx: e.dma_start(
            out=self.a5[:], in_=w[i]), deps=[P.a5g], inc="wld", n=16)
        tc = P.do("gpsimd", lambda e, d=dst: e.tensor_copy(
            d, self.a5[:].rearrange("p (c o) -> p c o", c=CC)),
            deps=[td] + list(cast_deps), inc="pool")
        P.a5g = tc
        return tc

    def load_qkv_group(self, g, deps):
        toks = []
        for part, base, dst, off in (("q", 2 * g, self.a7, 0),
                                     ("k", 8 + 2 * g, self.a7, 256),
                                     ("v", 16 + 2 * g, self.wv_g, 0)):
            for hh in range(2):
                t = self.load_slice(
                    self.wqkv, base + hh,
                    dst[:, :, off + 128 * hh:off + 128 * (hh + 1)], deps)
                deps = ()
            toks.append(t)
        return tuple(toks)

    def load_half(self, w_sliced, half, buf, deps):
        t = None
        for qt in range(4):
            t = self.load_slice(w_sliced, 4 * half + qt,
                                buf[:, :, 128 * qt:128 * (qt + 1)], deps)
            deps = ()
        return t

    def prelude(self):
        P = self.P
        for dst, src in ((self.bqkv_sb[:], self.bqkv_h),
                         (self.bout_sb[:], self.bout_h),
                         (self.g_sb[:], self.gln_h), (self.b_sb[:], self.bln_h),
                         (self.mur_sb[:], self.mur_h),
                         (self.muk_sb[:], self.muk_h),
                         (self.grel[:], self.wrel2_h),
                         (self.smf[:, 0:1], self.sgn_h)):
            P.do("sync", lambda e, d=dst, s=src: e.dma_start(out=d, in_=s),
                 inc="in", n=16)
        tin0 = ("in", P.cnt["in"])

        P.do("vector", lambda e: e.memset(self.ones128[:], 1.0))
        P.do("vector", lambda e: e.memset(self.onesm[:], 1.0 / 1024.0))
        P.do("vector", lambda e: e.memset(self.cst[:, 0:1], float(EPS)))
        P.do("vector", lambda e: e.memset(self.cst[:, 1:2], float(EPS * EPS)))
        P.do("vector", lambda e: e.memset(self.nsel[:], 0.0))
        P.do("vector", lambda e: e.memset(self.nsel[0:64, 0:1], 1.0))
        P.do("vector", lambda e: e.memset(self.nsel[64:128, 1:2], 1.0))
        P.do("vector", lambda e: e.memset(self.mh_t[:], 0.0))
        P.do("vector", lambda e: e.memset(self.a1[:, :, 0:1], 0.0))
        for v in self.vt:
            P.do("vector", lambda e, v=v: e.memset(v[:, :, 64:65], 1.0))
            P.do("vector", lambda e, v=v: e.memset(v[:, :, 97:98], 1.0))
            P.do("vector", lambda e, v=v: e.memset(v[:, :, 65:97], 0.0))
            P.do("vector", lambda e, v=v: e.memset(v[:, :, 98:129], 0.0))
        P.do("vector", lambda e: e.tensor_scalar_mul(
            self.grel[64:128, :], self.grel[64:128, :], self.smf[64:128, 0:1]),
            deps=[tin0])
        tng = P.do("vector", lambda e: e.tensor_scalar(
            self.babs[0:64, :], self.grel[0:64, :], -1.0, None, ALU.mult),
            deps=[tin0], inc="dve")
        tabs = P.do("vector", lambda e: e.tensor_tensor(
            self.babs[0:64, :], self.babs[0:64, :], self.grel[0:64, :],
            ALU.max), deps=[tng], inc="dve", strict=True)
        t = P.do("tensor", lambda e: e.matmul(
            self.ps[0:1, 0:16], self.ones128[0:64, 0:1], self.babs[0:64, :],
            start=True, stop=True), deps=[tabs], inc="pe")
        t = P.do("vector", lambda e: e.tensor_copy(
            self.mrep[0:1, 0:16], self.ps[0:1, 0:16]), deps=[t], inc="dve")
        t = P.do("tensor", lambda e: e.matmul(
            self.ps[0:128, 512:528], self.ones128[0:1, :],
            self.mrep[0:1, 0:16], start=True, stop=True), deps=[t], inc="pe")
        tbb = P.do("vector", lambda e: e.tensor_copy(
            self.bb[:], self.ps[0:128, 512:528]), deps=[t], inc="dve")
        P.do("vector", lambda e: e.memset(self.onesw[:], 1.0))
        P.do("vector", lambda e: e.memset(self.ident2[0:1, 0:1], 1.0))
        P.do("vector", lambda e: e.memset(self.ident2[0:1, 1:2], 0.0))
        P.do("sync", lambda e: e.dma_start(
            out=self.ident2[1:2, 0:1], in_=self.ident2[0:1, 1:2]),
            deps=[tbb], inc="in", n=16)
        P.do("sync", lambda e: e.dma_start(
            out=self.ident2[1:2, 1:2], in_=self.ident2[0:1, 0:1]),
            deps=[tbb], inc="in", n=16)
        P.do("sync", lambda e: e.dma_start(
            out=self.bbp[0:1, 0:8], in_=self.bb[0:1, 0:16:2],
            allow_slow_non_contiguous=True),
            deps=[tbb], inc="in", n=16)
        self.tbbp = P.do("sync", lambda e: e.dma_start(
            out=self.bbp[1:2, 0:8], in_=self.bb[1:2, 1:16:2],
            allow_slow_non_contiguous=True),
            deps=[tbb], inc="in", n=16)

        last = None
        for half in range(2):
            td = P.do("sync", lambda e, h=half: e.dma_start(
                out=self.a5[:], in_=self.trig_h[:, h * 1024:(h + 1) * 1024]),
                deps=[last] if last else [], inc="in", n=16)
            last = P.do("gpsimd",
                lambda e, h=half: e.tensor_copy(
                    self.cs[:, h * 1024:(h + 1) * 1024], self.a5[:, 0:1024]),
                deps=[td], inc="pool")
        td = P.do("sync", lambda e: e.dma_start(
            out=self.a5[:, 0:64], in_=self.dmask_h),
            deps=[last], inc="in", n=16)
        self.P.a5g = P.do("gpsimd", lambda e: e.tensor_copy(
            self.dm[:], self.a5[:, 0:64]), deps=[td], inc="pool")

    def ln_stats(self, sum_tile, sq_tile, sum_deps, sq_deps, site1,
                 m_dst=None):
        P, ps = self.P, self.ps
        m_dst = m_dst if m_dst is not None else self.mrep
        t_sum = t_sq = None
        for cc in range(CC):
            for j in range(4):
                t_sum = P.do("tensor", lambda e, cc=cc, j=j: e.matmul(
                    ps[0:1, j * 512:(j + 1) * 512], self.onesm[:],
                    sum_tile(cc)[:, j * 512:(j + 1) * 512],
                    start=(cc == 0), stop=(cc == CC - 1)),
                    deps=sum_deps[cc], inc="pe" if j == 3 else None)
                t_sq = P.do("tensor", lambda e, cc=cc, j=j: e.matmul(
                    ps[0:1, 2048 + j * 512:2048 + (j + 1) * 512], self.onesm[:],
                    sq_tile(cc)[:, j * 512:(j + 1) * 512],
                    start=(cc == 0), stop=(cc == CC - 1)),
                    deps=sq_deps[cc], inc="pe" if j == 3 else None)
        t2 = P.do("vector", lambda e, d=m_dst: e.tensor_copy(
            d[0:1, :], ps[0:1, 0:2048]), deps=[t_sum], inc="dve")
        tm = None
        for j in range(4):
            tm = P.do("tensor", lambda e, j=j, d=m_dst: e.matmul(
                ps[0:128, j * 512:(j + 1) * 512], self.ones128[0:1, :],
                d[0:1, j * 512:(j + 1) * 512], start=True, stop=True),
                deps=[t2], inc="pe" if j == 3 else None)
        tmr = P.do("vector", lambda e, d=m_dst: e.tensor_copy(
            d[:, :], ps[:, 0:2048]), deps=[tm], inc="dve")
        t3 = P.do("vector", lambda e, d=m_dst: e.tensor_tensor(
            self.rows[0:1, :], d[0:1, :], d[0:1, :], ALU.mult),
            deps=[t2], inc="dve")
        t4 = P.do("vector", lambda e: e.tensor_tensor(
            self.rows[0:1, :], ps[0:1, 2048:4096], self.rows[0:1, :],
            ALU.subtract), deps=[t3, t_sq], inc="dve")
        c1 = (1.0 + EPS) if site1 else 1.0
        c0 = (EPS * EPS) if site1 else EPS
        t5 = P.do("scalar", lambda e, c1=c1, c0=c0: e.activation(
            self.rows[0:1, :], self.rows[0:1, :], AF.Ln, bias=c0, scale=c1),
            deps=[t4, t2], inc="act")
        t6 = P.do("scalar", lambda e: e.activation(
            self.arep[0:1, :], self.rows[0:1, :], AF.Exp, scale=-0.5),
            deps=[t5, tm], inc="act")
        self.warm([t4])
        self.warm([t5])
        ta = None
        for j in range(4):
            ta = P.do("tensor", lambda e, j=j: e.matmul(
                ps[0:128, 2048 + j * 512:2048 + (j + 1) * 512],
                self.ones128[0:1, :],
                self.arep[0:1, j * 512:(j + 1) * 512], start=True, stop=True),
                deps=[t6], inc="pe" if j == 3 else None)
        tar = P.do("scalar", lambda e: e.activation(
            self.arep[:, :], ps[:, 2048:4096], AF.Copy), deps=[ta], inc="act")
        return tmr, tar

    def a4c(self, cc):
        return self.a4[:, 2048 * cc:2048 * (cc + 1)]

    def warm(self, deps):
        self.P.do("tensor", lambda e: e.matmul(
            self.ps[0:2, 1792:1794], self.onesw[0:2, 0:2],
            self.ident2[0:2, 0:2], start=True, stop=True), deps=deps)

    def site1(self, x_toks):
        P = self.P
        self.warm([x_toks[0]])
        sdeps, qdeps = [], []
        for cc in range(CC):
            tc = P.do("vector", lambda e, cc=cc: e.tensor_copy(
                self.a2[:, cc, :], self.x_sb[:, cc, :]),
                deps=[x_toks[cc // 2]], inc="dve")
            tq = P.do("scalar", lambda e, cc=cc: e.activation(
                self.a4c(cc), self.x_sb[:, cc, :], AF.Square),
                deps=[x_toks[cc // 2]], inc="act")
            sdeps.append([tc])
            qdeps.append([tq])
        tmr, tar = self.ln_stats(lambda cc: self.a2[:, cc, :], self.a4c,
                                 sdeps, qdeps, True, m_dst=self.mrep2)
        for half in range(2):
            td = P.do("sync", lambda e, h=half: e.dma_start(
                out=self.a5[:], in_=self.u2_h[:, h * 1024:(h + 1) * 1024]),
                deps=[P.a5g, tar], inc="wld", n=16)
            for dst, lo, hi in ((self.qs[0], 64, 128), (self.qs[1], 0, 64)):
                P.a5g = P.do("gpsimd",
                    lambda e, d=dst, lo=lo, hi=hi, h=half: e.tensor_copy(
                        d[lo:hi, h * 1024:(h + 1) * 1024],
                        self.a5[lo:hi, 0:1024]),
                    deps=[td], inc="pool")
        self.u_tok = P.a5g
        z_cc = []
        for cc in range(CC):
            P.do("vector", lambda e, cc=cc: e.tensor_tensor(
                self.z(cc), self.a2[:, cc, :], self.mrep2[:], ALU.subtract),
                deps=[tmr])
        for cc in range(CC):
            P.do("vector", lambda e, cc=cc: e.tensor_tensor(
                self.z(cc), self.z(cc), self.arep[:], ALU.mult),
                deps=[tar])
            z_cc.append(P.do("vector", lambda e, cc=cc: e.tensor_scalar(
                self.z(cc), self.z(cc), self.g_sb[:, cc:cc + 1],
                self.b_sb[:, cc:cc + 1], ALU.mult, ALU.add), inc="dve"))
        return z_cc

    def ln_residual(self, src, sum_deps, sq_tile, sq_deps, mirror=False,
                    warm_toks=(0,)):
        P = self.P
        tmr, tar = self.ln_stats(src, sq_tile, sum_deps, sq_deps, False)
        toks = []
        msq = []
        for cc in range(CC):
            P.do("vector", lambda e, cc=cc: e.tensor_tensor(
                src(cc), src(cc), self.mrep[:], ALU.subtract),
                deps=[tmr])
        for cc in range(CC):
            tm_ = P.do("vector", lambda e, cc=cc: e.tensor_tensor(
                src(cc), src(cc), self.arep[:], ALU.mult),
                deps=[tar], inc="dve")
            ta_ = P.do("vector", lambda e, cc=cc: e.tensor_tensor(
                self.x_sb[:, cc, :], self.x_sb[:, cc, :], src(cc), ALU.add),
                deps=[tm_], inc="dve")
            toks.append(ta_)
            if mirror:
                tq = P.do("scalar", lambda e, cc=cc: e.activation(
                    self.a2[:, cc, :], self.x_sb[:, cc, :], AF.Square),
                    deps=[ta_, tm_], inc="act")
                msq.append([tq])
        for cc in warm_toks:
            self.warm([toks[cc]])
        return toks[-1], toks, msq

    def lnd(self, msq):
        P, ps = self.P, self.ps
        t_sq = None
        for cc in range(CC):
            for j in range(4):
                t_sq = P.do("tensor", lambda e, cc=cc, j=j: e.matmul(
                    ps[0:1, 2048 + j * 512:2048 + (j + 1) * 512], self.onesm[:],
                    self.a2[:, cc, j * 512:(j + 1) * 512],
                    start=(cc == 0), stop=(cc == CC - 1)),
                    deps=msq[cc], inc="pe" if j == 3 else None)
        t3 = P.do("vector", lambda e: e.tensor_tensor(
            self.rows[0:1, :], self.mrep2[0:1, :], self.mrep2[0:1, :],
            ALU.mult), inc="dve")
        t4 = P.do("vector", lambda e: e.tensor_tensor(
            self.rows[0:1, :], ps[0:1, 2048:4096], self.rows[0:1, :],
            ALU.subtract), deps=[t3, t_sq], inc="dve")
        t5 = P.do("scalar", lambda e: e.activation(
            self.rows[0:1, :], self.rows[0:1, :], AF.Ln, bias=EPS, scale=1.0),
            deps=[t4], inc="act")
        t6 = P.do("scalar", lambda e: e.activation(
            self.arep[0:1, :], self.rows[0:1, :], AF.Exp, scale=-0.5),
            deps=[t5], inc="act")
        self.warm([t4])
        self.warm([t5])
        ta = None
        for j in range(4):
            ta = P.do("tensor", lambda e, j=j: e.matmul(
                ps[0:128, 2048 + j * 512:2048 + (j + 1) * 512],
                self.ones128[0:1, :],
                self.arep[0:1, j * 512:(j + 1) * 512], start=True, stop=True),
                deps=[t6], inc="pe" if j == 3 else None)
        tar = P.do("scalar", lambda e: e.activation(
            self.arep[:, :], ps[:, 2048:4096], AF.Copy), deps=[ta], inc="act")
        xln_cc = []
        for cc in range(CC):
            P.do("vector", lambda e, cc=cc: e.tensor_tensor(
                self.z(cc), self.x_sb[:, cc, :], self.mrep2[:], ALU.subtract),
                deps=[t_sq])
        for cc in range(CC):
            xln_cc.append(P.do("vector", lambda e, cc=cc: e.tensor_tensor(
                self.z(cc), self.z(cc), self.arep[:], ALU.mult),
                deps=[tar], inc="dve"))
        return xln_cc

    def attention(self, z_cc, att_tok):
        self.z_cc = z_cc
        """Returns final att pool token (att in a2). Weights for group 0 are
        preloaded (self.wtok_qkv); each group prefetches the next group's
        weights (and for g==3 the out-proj halves) during its second half."""
        P, ps = self.P, self.ps
        self.rrg = {}
        self.slot_g = {}
        for g in range(NG):
            init = att_tok or self.b_free
            self.att_i = 0
            self.texp_reg = [init, init, init]
            self.tpv_par = [init, init]
            self.out_guard = [init]
            twq, twk, twv = self.wtok_qkv
            tvev = None
            for ocg in range(2):
                tes = self.qk_mms_evacs(g, ocg, z_cc, att_tok, twq, twk)
                j0 = 2 * ocg
                tb3 = self.emit_B(g, j0, [att_tok, self.b_free,
                                          tes[2], tes[3]])
                if ocg == 0:
                    tvev = self.v_phase(g, twv, tes, 0, 8)
                tn, pr = self.norms_part("q", ocg, tes, None, None)
                if ocg == 0:
                    tvev = self.v_phase(g, twv, tes, 8, TB)
                tn, pr = self.norms_part("k", ocg, tes, pr, tn)
                pair_done = self.negm_pair(g, ocg, pr)
                if ocg == 1:
                    if g < NG - 1:
                        self.wtok_qkv = self.load_qkv_group(
                            g + 1, [pair_done])
                    else:
                        self.wtok_out = (
                            self.load_half(self.wout, 0, self.a7,
                                           [pair_done]),
                            self.load_half(self.wout, 1, self.wv_g, ()))
                att_tok, tb3n = self.head(
                    g, j0, tb3, att_tok, tvev, pair_done, red_done=pr,
                    tail=(g, j0 + 1))
                att_tok, _ = self.head(g, j0 + 1, tb3n, att_tok, tvev,
                                       pair_done, tail=None)
        return att_tok

    def qk_mms_evacs(self, g, ocg, z_cc, att_tok, twq, twk):
        P, ps = self.P, self.ps
        tq = tk = None
        for cc in range(CC):
            for j in range(4):
                tq = P.do("tensor", lambda e, cc=cc, j=j, ocg=ocg: e.matmul(
                    ps[0:128, j * 512:(j + 1) * 512],
                    self.a7[:, cc, 128 * ocg:128 * (ocg + 1)],
                    self.z(cc)[:, j * 512:(j + 1) * 512],
                    start=(cc == 0), stop=(cc == CC - 1)),
                    deps=[twq, z_cc[cc], att_tok],
                    inc="pe" if j == 3 else None)
        for cc in range(CC):
            for j in range(4):
                tk = P.do("tensor", lambda e, cc=cc, j=j, ocg=ocg: e.matmul(
                    ps[0:128, 2048 + j * 512:2048 + (j + 1) * 512],
                    self.a7[:, cc, 256 + 128 * ocg:256 + 128 * (ocg + 1)],
                    self.z(cc)[:, j * 512:(j + 1) * 512],
                    start=(cc == 0), stop=(cc == CC - 1)),
                    deps=[twk], inc="pe" if j == 3 else None)
        oq = 2 * g + ocg
        ok = 8 + oq
        te0 = P.do("vector", lambda e, oq=oq: e.tensor_scalar(
            self.qs[0][0:64, :], ps[0:64, 0:2048],
            self.bqkv_sb[0:64, oq:oq + 1], 0.125, ALU.add, ALU.mult),
            deps=[tq], inc="dve")
        te1 = P.do("vector", lambda e, oq=oq: e.tensor_scalar(
            self.qs[1][64:128, :], ps[64:128, 0:2048],
            self.bqkv_sb[64:128, oq:oq + 1], 0.125, ALU.add, ALU.mult),
            deps=[tq], inc="dve")
        te2 = P.do("scalar", lambda e, ok=ok: e.activation(
            self.ks[0][0:64, :], ps[0:64, 2048:4096], AF.Identity,
            bias=self.bqkv_sb[0:64, ok:ok + 1]), deps=[tk], inc="act")
        te3 = P.do("scalar", lambda e, ok=ok: e.activation(
            self.ks[1][64:128, :], ps[64:128, 2048:4096], AF.Identity,
            bias=self.bqkv_sb[64:128, ok:ok + 1]), deps=[tk], inc="act")
        return (te0, te1, te2, te3)

    def norms_part(self, which, ocg, tes, prev_red, tn_prev):
        P, ps = self.P, self.ps
        te0, te1, te2, te3 = tes
        if which == "q":
            col, slots, toks = 0, (self.qs[0], self.qs[1]), (te0, te1)
        else:
            col, slots, toks = 4, (self.ks[0], self.ks[1]), (te2, te3)
        tq0 = P.do("scalar", lambda e, sl=slots[0]: e.activation(
            self.nsq[0:64, :], sl[0:64, :], AF.Square),
            deps=[toks[0], tn_prev], inc="act")
        tq1 = P.do("scalar", lambda e, sl=slots[1]: e.activation(
            self.nsq[64:128, :], sl[64:128, :], AF.Square),
            deps=[toks[1]], inc="act")
        self.warm([tq0, prev_red])
        self.warm([tq1, prev_red])
        tn = None
        for j in range(4):
            tn = P.do("tensor", lambda e, j=j: e.matmul(
                ps[0:2, j * 512:(j + 1) * 512], self.nsel[:, 0:2],
                self.nsq[:, j * 512:(j + 1) * 512], start=True, stop=True),
                deps=[tq0, tq1, prev_red, te0, te1],
                inc="pe" if j == 3 else None)
        prev_red = P.do("vector", lambda e, col=col, ocg=ocg: e.reduce_max(
            self.mvals[0:2, col + ocg:col + ocg + 1],
            ps[0:2, 0:2048], AX), deps=[tn], inc="dve")
        return tn, prev_red

    def negm_pair(self, g, ocg, prev_red):
        P, ps = self.P, self.ps
        c = 2 * g + ocg
        j0 = 2 * ocg
        t1 = P.do("vector", lambda e, ocg=ocg, c=c: e.scalar_tensor_tensor(
            self.smf[0:2, 1:2], self.mvals[0:2, 4 + ocg:5 + ocg],
            -1.0 / 16.0, self.bbp[0:2, c:c + 1], ALU.mult, ALU.subtract),
            deps=[prev_red], inc="dve", strict=True)
        t4 = P.do("vector", lambda e, ocg=ocg: e.scalar_tensor_tensor(
            self.negmb2[0:2, 0:1], self.mvals[0:2, ocg:ocg + 1],
            -4.0, self.smf[0:2, 1:2], ALU.mult, ALU.add),
            deps=[t1], inc="dve", strict=True)
        t4c = P.do("vector", lambda e: e.tensor_scalar_mul(
            self.negval[0:2, :], self.onesw[0:2, :], self.negmb2[0:2, 0:1]),
            deps=[t4], inc="dve", strict=True)
        t5 = P.do("tensor", lambda e: e.matmul(
            ps[0:128, 3584:3586], self.negval[0:2, :],
            self.ident2[0:2, 0:2], start=True, stop=True),
            deps=[t4c], inc="pe")
        last = P.do("vector", lambda e, j0=j0: e.tensor_copy(
            self.negm[:, j0:j0 + 2], ps[0:128, 3584:3586]),
            deps=[t5], inc="dve")
        return last

    def v_phase(self, g, twv, tes, tb0, tb1):
        P, ps = self.P, self.ps
        te0, te1, te2, te3 = tes
        if tb0 == 0:
            for v in self.vt:
                P.do("vector", lambda e, v=v: e.memset(v[:, :, 64:65], 1.0),
                     deps=[te2, te3])
                P.do("vector", lambda e, v=v: e.memset(v[:, :, 97:98], 1.0))
                P.do("vector", lambda e, v=v: e.memset(v[:, :, 65:97], 0.0))
                P.do("vector", lambda e, v=v: e.memset(v[:, :, 98:129], 0.0))
        tvev = None
        for tb in range(tb0, tb1):
            reg = 2048 + (tb % 2) * 512
            tvz = None
            for cc in range(CC):
                tvz = P.do("tensor", lambda e, tb=tb, cc=cc, reg=reg: e.matmul(
                    ps[0:128, reg:reg + 256], self.zblk(cc, tb),
                    self.wv_g[:, cc, 0:256],
                    start=(cc == 0), stop=(cc == CC - 1)),
                    deps=[twv, te2, te3, self.z_cc[cc],
                          self._vwar.get(tb % 2)],
                    inc="pe" if cc == CC - 1 else None)
            for j in range(4):
                dst = self.vt[j // 2]
                off = 0 if j % 2 == 0 else 129
                tok = P.do("vector",
                    lambda e, tb=tb, j=j, reg=reg, dst=dst, off=off:
                    e.tensor_copy(dst[:, tb, off:off + 64],
                                  ps[:, reg + 64 * j:reg + 64 * (j + 1)]),
                    deps=[tvz], inc="dve" if j == 3 else None)
            self._vwar[tb % 2] = tok
            tvev = tok
        return tvev

    def emit_B(self, g, j, deps, col0=3072):
        P, ps = self.P, self.ps
        h = 4 * g + j
        par = j % 2
        ksl = self.ks[par]
        tm1 = P.do("vector", lambda e, h=h, par=par: e.tensor_scalar(
            self.mh_t[:, 64 * (1 - par):64 * (1 - par) + 64], self.dm[:],
            self.grel[:, h:h + 1], None, ALU.mult),
            deps=list(deps), inc="dve")
        rlo = 64 if par == 0 else 0
        tb3 = tm1
        for half in range(2):
            tbm = None
            for hh in range(2):
                tbm = P.do("tensor", lambda e, half=half, hh=hh, col0=col0:
                    e.matmul(
                        ps[0:128, col0 + hh * 512:col0 + (hh + 1) * 512],
                        self.mh_t[:],
                        self.cs[:, 1024 * half + 512 * hh:
                                1024 * half + 512 * (hh + 1)],
                        start=True, stop=True),
                    deps=[tb3], inc="pe" if hh == 1 else None)
            tb3 = P.do("vector", lambda e, half=half, rlo=rlo, ksl=ksl,
                              col0=col0: e.tensor_copy(
                    ksl[rlo:rlo + 64, 1024 * half:1024 * (half + 1)],
                    ps[rlo:rlo + 64, col0:col0 + 1024]),
                deps=[tbm], inc="dve")
        self.b_free = tb3
        return tb3

    def head(self, g, j, tb3, att_prev, tvev, pair_done, red_done=None,
             tail=None):
        P, ps = self.P, self.ps
        h = 4 * g + j
        par = j % 2
        qsl, ksl = self.qs[par], self.ks[par]
        R = (0, 1024, 3072)
        vview = self.vt[j // 2]
        lc = (0, 65) if par == 0 else (65, 193)
        drow = 64 if par == 0 else 32
        a, b = (0, 64) if par == 0 else (64, 128)
        att_tok = att_prev
        tb3n = None

        def gate(i, kb, qh):
            if red_done is None or qh != 0 or kb >= 3:
                return []
            return [pair_done] if (i % 3 == 2) else [red_done]

        def emit_S(i, qh, kb, extra):
            tmm = None
            for jj in range(2):
                tmm = P.do("tensor",
                    lambda e, i=i, kb=kb, jj=jj, qh=qh, qsl=qsl, ksl=ksl:
                    e.matmul(
                        ps[0:128, R[i % 3] + jj * 512:
                           R[i % 3] + (jj + 1) * 512],
                        ksl[:, 128 * kb:128 * (kb + 1)],
                        qsl[:, 1024 * qh + 512 * jj:
                            1024 * qh + 512 * (jj + 1)],
                        start=True, stop=True),
                    deps=[tb3, self.texp_reg[i % 3]] + extra,
                    inc="pe" if jj == 1 else None)
            return tmm

        for qh in range(2):
            i0 = self.att_i
            first = [self.u_tok] if (qh == 0) else []
            tmm_k = {}
            tmm_k[0] = emit_S(i0, qh, 0, first + gate(i0, 0, qh))
            tmm_k[1] = emit_S(i0 + 1, qh, 1, gate(i0 + 1, 1, qh))
            tpv = None
            for kb in range(TB):
                i = i0 + kb
                pr3 = i % 3
                pr2 = i % 2
                if kb + 2 < TB:
                    tmm_k[kb + 2] = emit_S(i + 2, qh, kb + 2,
                                           gate(i + 2, kb + 2, qh))
                texp = P.do("scalar",
                    lambda e, pr2=pr2, st0=R[pr3], j=j: e.activation(
                        self.pbuf[pr2][:], ps[:, st0:st0 + 1024], AF.Exp,
                        bias=self.negm[:, j:j + 1]),
                    deps=[tmm_k[kb], self.tpv_par[pr2]]
                         + ([pair_done] if (kb == 0 and qh == 0) else []),
                    inc="act")
                self.texp_reg[pr3] = texp
                for jj in range(2):
                    tpv = P.do("tensor",
                        lambda e, kb=kb, jj=jj, pr2=pr2, lc=lc, vview=vview:
                        e.matmul(
                            ps[0:lc[1] - lc[0],
                               2048 + jj * 512:2048 + (jj + 1) * 512],
                            vview[:, kb, lc[0]:lc[1]],
                            self.pbuf[pr2][:, jj * 512:(jj + 1) * 512],
                            start=(kb == 0), stop=(kb == TB - 1)),
                        deps=[texp] + (([tvev] + self.out_guard)
                                       if kb == 0 else []),
                        inc="pe" if jj == 1 else None)
                self.tpv_par[pr2] = tpv
            self.att_i = i0 + TB
            t1 = P.do("scalar", lambda e, drow=drow: e.activation(
                self.rows[drow:drow + 1, 0:1024],
                ps[drow:drow + 1, 2048:3072], AF.Ln), deps=[tpv], inc="act")
            t2 = P.do("scalar", lambda e, drow=drow: e.activation(
                self.rrow[drow:drow + 1, :],
                self.rows[drow:drow + 1, 0:1024], AF.Exp, scale=-1.0),
                deps=[t1, self.rrg.get(drow)], inc="act")
            rreg = (self.att_i + 2) % 3
            tr = None
            for jj in range(2):
                tr = P.do("tensor", lambda e, jj=jj, drow=drow, c0=R[rreg]:
                    e.matmul(
                        ps[0:128, c0 + jj * 512:c0 + (jj + 1) * 512],
                        self.ones128[drow:drow + 1, :],
                        self.rrow[drow:drow + 1, jj * 512:(jj + 1) * 512],
                        start=True, stop=True),
                    deps=[t2, self.texp_reg[rreg]],
                    inc="pe" if jj == 1 else None)
            self.rrg[drow] = tr
            slot = self.mrep[a:b, 1024 * qh:1024 * (qh + 1)]
            trr = P.do("vector", lambda e, a=a, b=b, c0=R[rreg], slot=slot:
                e.tensor_copy(slot, ps[a:b, c0:c0 + 1024]),
                deps=[tr, self.slot_g.get((par, qh))], inc="dve")
            self.texp_reg[rreg] = trr
            attv = self.a2[a:b, h // 2, 1024 * qh:1024 * (qh + 1)]
            tos = P.do("vector", lambda e, attv=attv, a=a, b=b: e.tensor_copy(
                attv, ps[a:b, 2048:3072]), deps=[tpv], inc="dve")
            self.out_guard = [tos, t1]
            if qh == 1 and tail is not None:
                tb3n = self.emit_B(tail[0], tail[1],
                                   [self.b_free, tos, t1], col0=2048)
            tp1 = P.do("vector", lambda e, attv=attv, slot=slot:
                e.tensor_tensor(attv, attv, slot, ALU.mult),
                deps=[trr, tos], inc="dve")
            att_tok = P.do("vector",
                lambda e, attv=attv, a=a, b=b, h=h: e.tensor_scalar(
                    attv, attv, self.bqkv_sb[a:b, 16 + h // 2:17 + h // 2],
                    None, ALU.add), deps=[tp1], inc="dve")
            self.slot_g[(par, qh)] = att_tok
        return att_tok, tb3n

    def out_proj(self, att_done):
        P, ps = self.P, self.ps
        tev = [None, None]
        pe_half = []
        sdeps, qdeps = [[] for _ in range(CC)], [[] for _ in range(CC)]
        for half in range(2):
            buf = self.a7 if half == 0 else self.wv_g
            ct = self.wtok_out[half]
            tmm = None
            for oc_ in range(4):
                oc = 4 * half + oc_
                reg = (oc_ % 2) * 2048
                for cc in range(CC):
                    for j in range(4):
                        tmm = P.do("tensor",
                            lambda e, cc=cc, j=j, oc_=oc_, reg=reg, buf=buf:
                            e.matmul(
                                ps[0:128, reg + j * 512:reg + (j + 1) * 512],
                                buf[:, cc, 128 * oc_:128 * (oc_ + 1)],
                                self.a2[:, cc, j * 512:(j + 1) * 512],
                                start=(cc == 0), stop=(cc == CC - 1)),
                            deps=[ct, att_done, tev[oc_ % 2]],
                            inc="pe" if j == 3 else None)
                tev[oc_ % 2] = P.do("vector",
                    lambda e, oc=oc, reg=reg: e.tensor_scalar(
                        self.z(oc), ps[:, reg:reg + 2048],
                        self.bout_sb[:, oc:oc + 1], None, ALU.add),
                    deps=[tmm], inc="dve")
                tsq = P.do("scalar", lambda e, oc=oc: e.activation(
                    self.a4c(oc), self.z(oc), AF.Square),
                    deps=[tev[oc_ % 2]], inc="act")
                sdeps[oc] = [tev[oc_ % 2]]
                qdeps[oc] = [tsq]
            pe_half.append(tmm)
        self.pe_half = pe_half
        sdeps[0] = sdeps[0] + [tev[0], tev[1]]
        qdeps[0] = qdeps[0] + [tev[0], tev[1]]
        return sdeps, qdeps

    def ffn(self, xln_cc):
        P, ps = self.P, self.ps
        xln = self.z
        xprev = lambda cc: self.a1[:, cc, 0:T]
        tk_cc = []
        for cc in range(CC):
            P.do("vector", lambda e, cc=cc: e.tensor_tensor(
                self.a2[:, cc, :], xln(cc), xprev(cc), ALU.subtract),
                deps=[xln_cc[cc]])
            tk_cc.append([P.do("vector", lambda e, cc=cc:
                e.scalar_tensor_tensor(
                    self.a2[:, cc, :], self.a2[:, cc, :],
                    self.muk_sb[:, cc:cc + 1], xprev(cc), ALU.mult, ALU.add),
                inc="dve")])

        ka_cc = [[] for _ in range(CC)]

        def relu_sq(dch, reg, tmm):
            t1 = P.do("vector", lambda e, dch=dch, reg=reg: e.tensor_scalar(
                self.ka_v[:, dch, :], ps[:, reg:reg + 2048], 0.0, None,
                ALU.max), deps=[tmm], inc="dve")
            tq = P.do("scalar", lambda e, dch=dch: e.activation(
                self.ka_v[:, dch, :], self.ka_v[:, dch, :], AF.Square),
                deps=[t1], inc="act")
            ka_cc[dch] = [tq]
            return tq

        rk = self.matmul_dd(self.wtok_wk, lambda cc: self.a2[:, cc, :], [],
                            relu_sq, rhs_deps_cc=tk_cc)
        self.wtok_wr = (self.load_half(self.wrt, 0, self.a7,
                                       [rk["pe_half"][0]]),
                        self.load_half(self.wrt, 1, self.wv_g,
                                       [rk["pe_half"][1]]))
        tr_cc = []
        for cc in range(CC):
            P.do("vector", lambda e, cc=cc: e.tensor_tensor(
                self.a2[:, cc, :], xln(cc), xprev(cc), ALU.subtract),
                deps=[xln_cc[cc], rk["mm_done"]])
            tr_cc.append([P.do("vector", lambda e, cc=cc:
                e.scalar_tensor_tensor(
                    self.a2[:, cc, :], self.a2[:, cc, :],
                    self.mur_sb[:, cc:cc + 1], xprev(cc), ALU.mult, ALU.add),
                inc="dve")])

        def sig(dch, reg, tmm):
            return P.do("scalar", lambda e, dch=dch, reg=reg: e.activation(
                self.a1[:, dch, 1:T + 1], ps[:, reg:reg + 2048], AF.Sigmoid),
                deps=[tmm], inc="act")

        rr = self.matmul_dd(self.wtok_wr, lambda cc: self.a2[:, cc, :], [],
                            sig, rhs_deps_cc=tr_cc)
        self.wtok_wv = (self.load_half(self.wvt, 0, self.a7,
                                       [rr["pe_half"][0]]),
                        self.load_half(self.wvt, 1, self.wv_g,
                                       [rr["pe_half"][1]]))
        sdeps, qdeps = [[] for _ in range(CC)], [[] for _ in range(CC)]

        def fmul(dch, reg, tmm):
            tf = P.do("vector", lambda e, dch=dch, reg=reg: e.tensor_tensor(
                self.a2[:, dch, :], ps[:, reg:reg + 2048],
                self.a1[:, dch, 1:T + 1], ALU.mult), deps=[tmm], inc="dve")
            tq = P.do("scalar", lambda e, dch=dch: e.activation(
                self.a1[:, dch, 1:T + 1], self.a2[:, dch, :], AF.Square),
                deps=[tf], inc="act")
            sdeps[dch] = [tf]
            qdeps[dch] = [tq]
            return tq

        rv = self.matmul_dd(self.wtok_wv, lambda cc: self.ka_v[:, cc, :],
                            [rr["evac_done"]], fmul, rhs_deps_cc=ka_cc)
        self.pe_half = rv["pe_half"]
        sdeps[0] = sdeps[0] + [sdeps[6][0], sdeps[7][0]]
        qdeps[0] = qdeps[0] + [sdeps[6][0], sdeps[7][0]]
        return sdeps, qdeps

    def matmul_dd(self, wtoks, rhs, rhs_deps, evac, rhs_deps_cc=None):
        P, ps = self.P, self.ps
        tevs = [None, None]
        mm_done = None
        ev_last = None
        pe_half = []
        for half in range(2):
            buf = self.a7 if half == 0 else self.wv_g
            ct = wtoks[half]
            tmm = None
            for d_ in range(4):
                dch = 4 * half + d_
                reg = (d_ % 2) * 2048
                for cc in range(CC):
                    dcc = rhs_deps_cc[cc] if rhs_deps_cc else []
                    for j in range(4):
                        tmm = P.do("tensor",
                            lambda e, cc=cc, j=j, d_=d_, reg=reg, buf=buf:
                            e.matmul(
                                ps[0:128, reg + j * 512:reg + (j + 1) * 512],
                                buf[:, cc, 128 * d_:128 * (d_ + 1)],
                                rhs(cc)[:, j * 512:(j + 1) * 512],
                                start=(cc == 0), stop=(cc == CC - 1)),
                            deps=list(rhs_deps) + dcc + [ct, tevs[d_ % 2]],
                            inc="pe" if j == 3 else None)
                mm_done = tmm
                tevs[d_ % 2] = evac(dch, reg, tmm)
                ev_last = tevs[d_ % 2]
            pe_half.append(tmm)
        return {"mm_done": mm_done, "evac_done": ev_last, "pe_half": pe_half}

    def build(self):
        P = self.P
        x_toks = []
        for i in range(4):
            x_toks.append(P.do("sync", lambda e, i=i: e.dma_start(
                out=self.x_sb[:, 2 * i:2 * i + 2, :],
                in_=self.x_hbm[0].rearrange(
                    "(c p) t -> p c t", p=128)[:, 2 * i:2 * i + 2, :]),
                inc="xi%d" % i, n=16))
        self.prelude()
        self._vwar = {}
        self.b_free = self.P.a5g
        self.u_tok = None
        store_toks = None
        att_tok = None
        self.wtok_qkv = self.load_qkv_group(0, [])
        for s in range(NSEQ):
            if s > 0:
                x_toks = []
                for i in range(4):
                    x_toks.append(P.do("sync", lambda e, s=s, i=i: e.dma_start(
                        out=self.x_sb[:, 2 * i:2 * i + 2, :],
                        in_=self.x_hbm[s].rearrange(
                            "(c p) t -> p c t", p=128)[:, 2 * i:2 * i + 2, :]),
                        deps=[store_toks[i]],
                        inc="xi%d" % i, n=16))
                self.warm([store_toks[0]])
            self.cur_seq = s
            z_cc = self.site1(x_toks)
            if s == 0:
                self.dump("z", self.a1[:], z_cc[-1])
            att_tok = self.attention(z_cc, att_tok)
            if s == 0:
                self.dump("att", self.a2[:], att_tok)
            sd_c, qd_c = self.out_proj(att_tok)
            self.wtok_wk = (self.load_half(self.wkt, 0, self.a7,
                                           [self.pe_half[0]]),
                            self.load_half(self.wkt, 1, self.wv_g,
                                           [self.pe_half[1]]))
            t_x1, _, msq = self.ln_residual(
                self.z, sd_c, self.a4c, qd_c, mirror=True)
            if s == 0:
                self.dump("x1", self.x_sb[:], t_x1)
            xln_cc = self.lnd(msq)
            if s == 0:
                self.dump("xln", self.a1[:], xln_cc[-1])
            sd_e, qd_e = self.ffn(xln_cc)
            if s + 1 < NSEQ:
                self.wtok_qkv = self.load_qkv_group(
                    0, [self.pe_half[0], self.pe_half[1]])
            t_x2, toks, _ = self.ln_residual(
                lambda cc: self.a2[:, cc, :], sd_e,
                lambda cc: self.a1[:, cc, 1:T + 1], qd_e, mirror=False,
                warm_toks=(0, 1))

            st_toks = []
            for i in range(4):
                st_toks.append(P.do("gpsimd", lambda e, s=s, i=i: e.dma_start(
                    out=self.out_hbm[s].rearrange(
                        "(c p) t -> p c t", p=128)[:, 2 * i:2 * i + 2, :],
                    in_=self.x_sb[:, 2 * i:2 * i + 2, :]),
                    deps=[toks[2 * i], toks[2 * i + 1]],
                    inc="so%d" % i, n=16))
            store_toks = st_toks

        for i in range(4):
            fv = P.cnt["so%d" % i]
            P.q["gpsimd"].append(
                lambda e, v=fv, i=i: e.wait_ge(P.sems["so%d" % i], v))
        if P.cnt["out"]:
            fv = P.cnt["out"]
            P.q["gpsimd"].append(lambda e, v=fv: e.wait_ge(P.sems["out"], v))


def build_program():
    from contextlib import ExitStack
    nc = bass.Bass("TRN2", target_bir_lowering=False)
    P = Prog()
    with ExitStack() as stack:
        k = K(nc, P, stack)
        k.build()
        with nc.Block() as block:
            @block.sync
            def _(e):
                for f in P.q["sync"]:
                    f(e)

            @block.tensor
            def _(e):
                for f in P.q["tensor"]:
                    f(e)

            @block.scalar
            def _(e):
                for f in P.q["scalar"]:
                    f(e)

            @block.vector
            def _(e):
                for f in P.q["vector"]:
                    f(e)

            @block.gpsimd
            def _(e):
                for f in P.q["gpsimd"]:
                    f(e)
    return nc


def repack_w(w):
    w = np.asarray(w, np.float32)
    n = w.shape[1] // 128
    t = w.reshape(CC, 128, n, 128)
    return np.ascontiguousarray(t.transpose(2, 1, 0, 3)).reshape(n, 128, 1024)


def prep_in_maps(inputs):
    x = np.asarray(inputs["L0_attn_x"])
    wqkv = repack_w(inputs["L0_attn_w_qkv"])
    wout = repack_w(inputs["L0_attn_w_out"])
    wrt = repack_w(np.asarray(inputs["L0_ffn_W_r"], np.float32).T)
    wkt = repack_w(np.asarray(inputs["L0_ffn_W_k"], np.float32).T)
    wvt = repack_w(np.asarray(inputs["L0_ffn_W_v"], np.float32).T)
    col = lambda v, n: np.ascontiguousarray(
        np.asarray(v, np.float32).reshape(n, 128).T)
    bqkv = col(inputs["L0_attn_b_qkv"], 24)
    bout = col(inputs["L0_attn_b_out"], 8)
    gln = col(inputs["L0_attn_ln_g"], 8)
    bln = col(inputs["L0_attn_ln_b"], 8)
    mur = col(inputs["L0_ffn_mu_r"], 8)
    muk = col(inputs["L0_ffn_mu_k"], 8)
    w_rel = np.asarray(inputs["L0_attn_w_rel"], np.float32)
    wsw = np.ascontiguousarray(
        w_rel.reshape(32, 2, 16)[:, ::-1, :].reshape(64, 16))
    wrel2 = np.ascontiguousarray(np.concatenate([w_rel, wsw], axis=0))
    u2, trig, dmask, sgn = host_constants()

    shared = {"wqkv": wqkv, "wout": wout, "wrt": wrt, "wkt": wkt, "wvt": wvt,
              "bqkv": bqkv, "bout": bout, "gln": gln, "bln": bln,
              "mur": mur, "muk": muk, "wrel2": wrel2, "sgn": sgn,
              "u2": u2, "trig": trig, "dmask": dmask}
    in_maps = []
    for core in range(8):
        xs = np.ascontiguousarray(
            x[2 * core:2 * core + 2].transpose(0, 2, 1)).astype(np.float32)
        m = dict(shared)
        m["x"] = xs
        in_maps.append(m)
    return in_maps


def kernel(**inputs):
    in_maps = prep_in_maps(inputs)
    nc = build_program()
    res = run_bass_kernel_spmd(nc, in_maps, core_ids=list(range(8)))
    outs = [np.asarray(res.results[c]["out"]).transpose(0, 2, 1)
            for c in range(8)]
    return np.concatenate(outs, axis=0).astype(np.float32)


# version 28
# speedup vs baseline: 1.0109x; 1.0109x over previous
import math
import numpy as np
import concourse.bass as bass
import concourse.mybir as mybir
from concourse.bass_utils import run_bass_kernel_spmd

F32 = mybir.dt.float32
BF16 = mybir.dt.bfloat16
AF = mybir.ActivationFunctionType
ALU = mybir.AluOpType
AX = mybir.AxisListType.X

T = 2048
D = 1024
H = 16
HD = 64
CC = 8
TB = 16
NG = 4
NSEQ = 2
EPS = 1e-5

ENGS = ("sync", "tensor", "scalar", "vector", "gpsimd")
ENG_SEM = {"tensor": "pe", "scalar": "act", "vector": "dve", "gpsimd": "pool"}

DEBUG_TAGS = set()


class Prog:
    def __init__(self):
        self.q = {e: [] for e in ENGS}
        self.cnt = {s: 0 for s in ("in", "wld", "out", "pe", "act",
                                   "dve", "pool", "rcp",
                                   "xi0", "xi1", "xi2", "xi3",
                                   "so0", "so1", "so2", "so3")}
        self.waited = {}
        self.sems = None
        self.a5g = None
        self.a7g = None

    def _wait(self, eng, tok, strict=False):
        if tok is None:
            return
        sem, v = tok
        if ENG_SEM.get(eng) == sem and not strict:
            return
        key = (eng, sem)
        if self.waited.get(key, 0) >= v:
            return
        self.waited[key] = v
        self.trace[eng].append(("wait", sem, v))
        self.q[eng].append(lambda e, sn=sem, vv=v: e.wait_ge(self.sems[sn], vv))

    trace = None

    def do(self, eng, fn, deps=(), inc=None, n=1, strict=False,
           raw_inc=False):
        if self.trace is None:
            self.trace = {e: [] for e in ENGS}
        for t in deps:
            self._wait(eng, t, strict=strict)
        if inc is None:
            self.trace[eng].append(("op", None, 0))
            self.q[eng].append(lambda e, f=fn: f(e))
            return None
        if raw_inc:
            v = None
        else:
            self.cnt[inc] += n
            v = self.cnt[inc]
        self.trace[eng].append(("op", inc, n))
        self.q[eng].append(
            lambda e, f=fn, sn=inc, nn=n: f(e).then_inc(self.sems[sn], nn))
        return (inc, v)


def host_constants():
    i = np.arange(0, HD, 2, dtype=np.float64)
    f = np.exp(i * (-math.log(10000.0) / HD))
    qq = np.arange(T, dtype=np.float64)
    sin = np.sin(qq[:, None] * f[None, :])
    cos = np.cos(qq[:, None] * f[None, :])
    U = np.zeros((T, HD))
    U[:, 0::2] = sin
    U[:, 1::2] = cos
    Ct = np.zeros((T, HD))
    Ct[:, 0::2] = cos
    Ct[:, 1::2] = cos
    St = np.zeros((T, HD))
    St[:, 0::2] = sin
    St[:, 1::2] = sin
    u2 = np.concatenate([U.T, U.T], axis=0).astype(np.float32)
    trig = np.concatenate([Ct.T, St.T], axis=0).astype(np.float32)
    dmask = np.zeros((128, 64), np.float32)
    for c in range(64):
        dmask[c, c] = 1.0
        dmask[64 + c, c] = 1.0
    sgn = np.ones((128, 1), np.float32)
    sgn[64:, 0] = np.tile([1.0, -1.0], 32)
    return u2, trig, dmask, sgn


class K:

    def __init__(self, nc, P, stack):
        self.nc, self.P = nc, P
        sb = lambda name, shape, dt: stack.enter_context(
            nc.sbuf_tensor(name, shape, dt))
        self.x_hbm = nc.dram_tensor("x", [NSEQ, D, T], F32,
                                    kind="ExternalInput").ap()
        self.out_hbm = nc.dram_tensor("out", [NSEQ, D, T], F32,
                                      kind="ExternalOutput").ap()
        dram_in = lambda name, shape: nc.dram_tensor(
            name, shape, F32, kind="ExternalInput").ap()
        self.wqkv = dram_in("wqkv", [24, 128, 1024])
        self.wout = dram_in("wout", [8, 128, 1024])
        self.wrt = dram_in("wrt", [8, 128, 1024])
        self.wkt = dram_in("wkt", [8, 128, 1024])
        self.wvt = dram_in("wvt", [8, 128, 1024])
        self.bqkv_h = dram_in("bqkv", [128, 24])
        self.bout_h = dram_in("bout", [128, 8])
        self.gln_h = dram_in("gln", [128, 8])
        self.bln_h = dram_in("bln", [128, 8])
        self.mur_h = dram_in("mur", [128, 8])
        self.muk_h = dram_in("muk", [128, 8])
        self.wrel2_h = dram_in("wrel2", [128, 16])
        self.sgn_h = dram_in("sgn", [128, 1])
        self.u2_h = dram_in("u2", [128, T])
        self.trig_h = dram_in("trig", [128, T])
        self.dmask_h = dram_in("dmask", [128, 64])

        self.x_sb = sb("x_sb", [128, CC, T], F32)
        self.a1 = sb("a1", [128, CC, T + 1], BF16)
        self.a2 = sb("a2", [128, CC, T], BF16)
        self.a4 = sb("a4", [128, 16640], BF16)
        self.a5 = sb("a5", [128, 1024], F32)
        self.a7 = sb("a7", [128, CC, 512], BF16)
        self.wv_g = sb("wv_g", [128, CC, 512], BF16)
        self.rows = sb("rows", [128, 2048], F32)
        self.mrep = sb("mrep", [128, T], BF16)
        self.mrep2 = sb("mrep2", [128, T], BF16)
        self.arep = sb("arep", [128, T], BF16)
        self.cs = sb("cs", [128, T], BF16)
        self.ones128 = sb("ones128", [128, 128], BF16)
        self.onesm = sb("onesm", [128, 1], BF16)
        self.nsel = sb("nsel", [128, 33], BF16)
        self.mh_t = sb("mh_t", [128, 128], BF16)
        self.grel = sb("grel", [128, 16], F32)
        self.dm = sb("dm", [128, 64], BF16)
        self.bb = sb("bb", [128, 16], F32)
        self.negm = sb("negm", [128, 4], F32)
        self.negmb = sb("negmb", [128, 4], BF16)
        self.negmb2 = sb("negmb2", [2, 1], F32)
        self.negval = sb("negval", [2, 128], BF16)
        self.onesw = sb("onesw", [2, 128], BF16)
        self.ident2 = sb("ident2", [2, 2], BF16)
        self.bbp = sb("bbp", [2, 8], F32)
        self.mvals = sb("mvals", [128, 8], F32)
        self.smf = sb("smf", [128, 4], F32)
        self.bqkv_sb = sb("bqkv_sb", [128, 24], F32)
        self.bqs = sb("bqs", [128, 8], F32)
        self.bout_sb = sb("bout_sb", [128, 8], F32)
        self.g_sb = sb("g_sb", [128, 8], F32)
        self.b_sb = sb("b_sb", [128, 8], F32)
        self.mur_sb = sb("mur_sb", [128, 8], F32)
        self.muk_sb = sb("muk_sb", [128, 8], F32)
        self.babs = sb("babs", [128, 16], BF16)
        self.cst = sb("cst", [128, 2], F32)
        nc.const_aps.register(F32, float(EPS), self.cst[:, 0:1])
        nc.const_aps.register(F32, float(EPS * EPS), self.cst[:, 1:2])
        self.ps = stack.enter_context(nc.psum_tensor("ps", [128, 4096], F32))

        self.rrow = self.rows[:, 1024:1536].bitcast(BF16)
        a4 = self.a4
        self.qs = [a4[:, 0:2048], a4[:, 2048:4096]]
        self.ks = [a4[:, 4096:6144], a4[:, 6144:8192]]
        self.vt = [a4[:, 8192:11392].rearrange("p (tb c) -> p tb c", tb=16),
                   a4[:, 11392:14592].rearrange("p (tb c) -> p tb c", tb=16)]
        self.pbuf = [a4[:, 14592:15616], a4[:, 15616:16640]]
        self.nsq = a4[:, 14592:16640]
        self.sc = [a4[:, 8192:10240], a4[:, 10240:12288]]
        self.ka_v = a4[:, 0:16384].rearrange("p (c t) -> p c t", c=CC)
        self.du_t = a4[:, 14592:16640]

        self.sems = {}
        for s in ("in", "wld", "out", "pe", "act", "dve", "pool", "rcp",
                  "xi0", "xi1", "xi2", "xi3", "so0", "so1", "so2", "so3"):
            self.sems[s] = stack.enter_context(nc.semaphore("sem_" + s))
        P.sems = self.sems

    def dump(self, tag, ap, dep, dtype=None):
        if tag not in DEBUG_TAGS:
            return
        shape = list(ap.shape)
        dt = dtype or ap.tensor.dtype
        d = self.nc.dram_tensor("dbg_" + tag, shape, dt,
                                kind="ExternalOutput").ap()
        self.P.do("gpsimd", lambda e: e.dma_start(out=d, in_=ap),
                  deps=[dep], inc="out", n=16)

    def z(self, cc):
        return self.a1[:, cc, 1:T + 1]

    def zblk(self, cc, tb):
        return self.a1[:, cc, 1 + 128 * tb:1 + 128 * (tb + 1)]

    def load_slice(self, w_sliced, idx, dst, cast_deps=()):
        P = self.P
        td = P.do("sync", lambda e, w=w_sliced, i=idx: e.dma_start(
            out=self.a5[:], in_=w[i]), deps=[P.a5g], inc="wld", n=16)
        tc = P.do("gpsimd", lambda e, d=dst: e.tensor_copy(
            d, self.a5[:].rearrange("p (c o) -> p c o", c=CC)),
            deps=[td] + list(cast_deps), inc="pool")
        P.a5g = tc
        return tc

    def load_qkv_group(self, g, deps):
        toks = []
        for part, base, dst, off in (("q", 2 * g, self.a7, 0),
                                     ("k", 8 + 2 * g, self.a7, 256),
                                     ("v", 16 + 2 * g, self.wv_g, 0)):
            for hh in range(2):
                t = self.load_slice(
                    self.wqkv, base + hh,
                    dst[:, :, off + 128 * hh:off + 128 * (hh + 1)], deps)
                deps = ()
            toks.append(t)
        return tuple(toks)

    def load_half(self, w_sliced, half, buf, deps):
        t = None
        for qt in range(4):
            t = self.load_slice(w_sliced, 4 * half + qt,
                                buf[:, :, 128 * qt:128 * (qt + 1)], deps)
            deps = ()
        return t

    def prelude(self):
        P = self.P
        for dst, src in ((self.bqkv_sb[:], self.bqkv_h),
                         (self.bout_sb[:], self.bout_h),
                         (self.g_sb[:], self.gln_h), (self.b_sb[:], self.bln_h),
                         (self.mur_sb[:], self.mur_h),
                         (self.muk_sb[:], self.muk_h),
                         (self.grel[:], self.wrel2_h),
                         (self.smf[:, 0:1], self.sgn_h)):
            P.do("sync", lambda e, d=dst, s=src: e.dma_start(out=d, in_=s),
                 inc="in", n=16)
        tin0 = ("in", P.cnt["in"])

        P.do("vector", lambda e: e.memset(self.ones128[:], 1.0))
        P.do("vector", lambda e: e.memset(self.onesm[:], 1.0 / 1024.0))
        P.do("vector", lambda e: e.memset(self.cst[:, 0:1], float(EPS)))
        P.do("vector", lambda e: e.memset(self.cst[:, 1:2], float(EPS * EPS)))
        P.do("vector", lambda e: e.memset(self.nsel[:], 0.0))
        P.do("vector", lambda e: e.memset(self.nsel[0:64, 0:1], 1.0))
        P.do("vector", lambda e: e.memset(self.nsel[64:128, 1:2], 1.0))
        P.do("vector", lambda e: e.memset(self.mh_t[:], 0.0))
        P.do("vector", lambda e: e.memset(self.a1[:, :, 0:1], 0.0))
        for v in self.vt:
            P.do("vector", lambda e, v=v: e.memset(v[:, :, 64:65], 1.0))
            P.do("vector", lambda e, v=v: e.memset(v[:, :, 97:98], 1.0))
            P.do("vector", lambda e, v=v: e.memset(v[:, :, 65:97], 0.0))
            P.do("vector", lambda e, v=v: e.memset(v[:, :, 98:129], 0.0))
        P.do("vector", lambda e: e.tensor_scalar_mul(
            self.grel[64:128, :], self.grel[64:128, :], self.smf[64:128, 0:1]),
            deps=[tin0])
        tng = P.do("vector", lambda e: e.tensor_scalar(
            self.babs[0:64, :], self.grel[0:64, :], -1.0, None, ALU.mult),
            deps=[tin0], inc="dve")
        tabs = P.do("vector", lambda e: e.tensor_tensor(
            self.babs[0:64, :], self.babs[0:64, :], self.grel[0:64, :],
            ALU.max), deps=[tng], inc="dve", strict=True)
        t = P.do("tensor", lambda e: e.matmul(
            self.ps[0:1, 0:16], self.ones128[0:64, 0:1], self.babs[0:64, :],
            start=True, stop=True), deps=[tabs], inc="pe")
        t = P.do("vector", lambda e: e.tensor_copy(
            self.mrep[0:1, 0:16], self.ps[0:1, 0:16]), deps=[t], inc="dve")
        t = P.do("tensor", lambda e: e.matmul(
            self.ps[0:128, 512:528], self.ones128[0:1, :],
            self.mrep[0:1, 0:16], start=True, stop=True), deps=[t], inc="pe")
        tbb = P.do("vector", lambda e: e.tensor_copy(
            self.bb[:], self.ps[0:128, 512:528]), deps=[t], inc="dve")
        P.do("vector", lambda e: e.memset(self.onesw[:], 1.0))
        P.do("vector", lambda e: e.memset(self.ident2[0:1, 0:1], 1.0))
        P.do("vector", lambda e: e.memset(self.ident2[0:1, 1:2], 0.0))
        P.do("sync", lambda e: e.dma_start(
            out=self.ident2[1:2, 0:1], in_=self.ident2[0:1, 1:2]),
            deps=[tbb], inc="in", n=16)
        P.do("sync", lambda e: e.dma_start(
            out=self.ident2[1:2, 1:2], in_=self.ident2[0:1, 0:1]),
            deps=[tbb], inc="in", n=16)
        P.do("sync", lambda e: e.dma_start(
            out=self.bbp[0:1, 0:8], in_=self.bb[0:1, 0:16:2],
            allow_slow_non_contiguous=True),
            deps=[tbb], inc="in", n=16)
        self.tbbp = P.do("sync", lambda e: e.dma_start(
            out=self.bbp[1:2, 0:8], in_=self.bb[1:2, 1:16:2],
            allow_slow_non_contiguous=True),
            deps=[tbb], inc="in", n=16)

        last = None
        for half in range(2):
            td = P.do("sync", lambda e, h=half: e.dma_start(
                out=self.a5[:], in_=self.trig_h[:, h * 1024:(h + 1) * 1024]),
                deps=[last] if last else [], inc="in", n=16)
            last = P.do("gpsimd",
                lambda e, h=half: e.tensor_copy(
                    self.cs[:, h * 1024:(h + 1) * 1024], self.a5[:, 0:1024]),
                deps=[td], inc="pool")
        td = P.do("sync", lambda e: e.dma_start(
            out=self.a5[:, 0:64], in_=self.dmask_h),
            deps=[last], inc="in", n=16)
        self.P.a5g = P.do("gpsimd", lambda e: e.tensor_copy(
            self.dm[:], self.a5[:, 0:64]), deps=[td], inc="pool")

    def ln_stats(self, sum_tile, sq_tile, sum_deps, sq_deps, site1,
                 m_dst=None):
        P, ps = self.P, self.ps
        m_dst = m_dst if m_dst is not None else self.mrep
        t_sum = t_sq = None
        for cc in range(CC):
            for j in range(4):
                t_sum = P.do("tensor", lambda e, cc=cc, j=j: e.matmul(
                    ps[0:1, j * 512:(j + 1) * 512], self.onesm[:],
                    sum_tile(cc)[:, j * 512:(j + 1) * 512],
                    start=(cc == 0), stop=(cc == CC - 1)),
                    deps=sum_deps[cc], inc="pe" if j == 3 else None)
                t_sq = P.do("tensor", lambda e, cc=cc, j=j: e.matmul(
                    ps[0:1, 2048 + j * 512:2048 + (j + 1) * 512], self.onesm[:],
                    sq_tile(cc)[:, j * 512:(j + 1) * 512],
                    start=(cc == 0), stop=(cc == CC - 1)),
                    deps=sq_deps[cc], inc="pe" if j == 3 else None)
        t2 = P.do("vector", lambda e, d=m_dst: e.tensor_copy(
            d[0:1, :], ps[0:1, 0:2048]), deps=[t_sum], inc="dve")
        tm = None
        for j in range(4):
            tm = P.do("tensor", lambda e, j=j, d=m_dst: e.matmul(
                ps[0:128, j * 512:(j + 1) * 512], self.ones128[0:1, :],
                d[0:1, j * 512:(j + 1) * 512], start=True, stop=True),
                deps=[t2], inc="pe" if j == 3 else None)
        tmr = P.do("vector", lambda e, d=m_dst: e.tensor_copy(
            d[:, :], ps[:, 0:2048]), deps=[tm], inc="dve")
        t3 = P.do("vector", lambda e, d=m_dst: e.tensor_tensor(
            self.rows[0:1, :], d[0:1, :], d[0:1, :], ALU.mult),
            deps=[t2], inc="dve")
        t4 = P.do("vector", lambda e: e.tensor_tensor(
            self.rows[0:1, :], ps[0:1, 2048:4096], self.rows[0:1, :],
            ALU.subtract), deps=[t3, t_sq], inc="dve")
        c1 = (1.0 + EPS) if site1 else 1.0
        c0 = (EPS * EPS) if site1 else EPS
        t5 = P.do("scalar", lambda e, c1=c1, c0=c0: e.activation(
            self.rows[0:1, :], self.rows[0:1, :], AF.Ln, bias=c0, scale=c1),
            deps=[t4, t2], inc="act")
        t6 = P.do("scalar", lambda e: e.activation(
            self.arep[0:1, :], self.rows[0:1, :], AF.Exp, scale=-0.5),
            deps=[t5, tm], inc="act")
        self.warm([t4])
        self.warm([t5])
        ta = None
        for j in range(4):
            ta = P.do("tensor", lambda e, j=j: e.matmul(
                ps[0:128, 2048 + j * 512:2048 + (j + 1) * 512],
                self.ones128[0:1, :],
                self.arep[0:1, j * 512:(j + 1) * 512], start=True, stop=True),
                deps=[t6], inc="pe" if j == 3 else None)
        tar = P.do("scalar", lambda e: e.activation(
            self.arep[:, :], ps[:, 2048:4096], AF.Copy), deps=[ta], inc="act")
        return tmr, tar

    def a4c(self, cc):
        return self.a4[:, 2048 * cc:2048 * (cc + 1)]

    def texp_reg_set(self, c0, tok):
        self.texp_reg[(0, 1024, 3072).index(c0)] = tok

    def warm(self, deps):
        self.P.do("tensor", lambda e: e.ldweights(self.ones128[0:1, :]),
                  deps=deps)

    def site1(self, x_toks):
        P = self.P
        self.warm([x_toks[0]])
        sdeps, qdeps = [], []
        for cc in range(CC):
            tc = P.do("vector", lambda e, cc=cc: e.tensor_copy(
                self.a2[:, cc, :], self.x_sb[:, cc, :]),
                deps=[x_toks[cc // 2]], inc="dve")
            tq = P.do("scalar", lambda e, cc=cc: e.activation(
                self.a4c(cc), self.x_sb[:, cc, :], AF.Square),
                deps=[x_toks[cc // 2]], inc="act")
            sdeps.append([tc])
            qdeps.append([tq])
        tmr, tar = self.ln_stats(lambda cc: self.a2[:, cc, :], self.a4c,
                                 sdeps, qdeps, True, m_dst=self.mrep2)
        for half in range(2):
            td = P.do("sync", lambda e, h=half: e.dma_start(
                out=self.a5[:], in_=self.u2_h[:, h * 1024:(h + 1) * 1024]),
                deps=[P.a5g, tar], inc="wld", n=16)
            for dst, lo, hi in ((self.qs[0], 64, 128), (self.qs[1], 0, 64)):
                P.a5g = P.do("gpsimd",
                    lambda e, d=dst, lo=lo, hi=hi, h=half: e.tensor_copy(
                        d[lo:hi, h * 1024:(h + 1) * 1024],
                        self.a5[lo:hi, 0:1024]),
                    deps=[td], inc="pool")
        self.u_tok = P.a5g
        z_cc = []
        for cc in range(CC):
            P.do("vector", lambda e, cc=cc: e.tensor_tensor(
                self.z(cc), self.a2[:, cc, :], self.mrep2[:], ALU.subtract),
                deps=[tmr])
        for cc in range(CC):
            P.do("vector", lambda e, cc=cc: e.tensor_tensor(
                self.z(cc), self.z(cc), self.arep[:], ALU.mult),
                deps=[tar])
            z_cc.append(P.do("vector", lambda e, cc=cc: e.tensor_scalar(
                self.z(cc), self.z(cc), self.g_sb[:, cc:cc + 1],
                self.b_sb[:, cc:cc + 1], ALU.mult, ALU.add), inc="dve"))
        return z_cc

    def ln_residual(self, src, sum_deps, sq_tile, sq_deps, mirror=False,
                    warm_toks=(0,)):
        P = self.P
        tmr, tar = self.ln_stats(src, sq_tile, sum_deps, sq_deps, False)
        toks = []
        msq = []
        for cc in range(CC):
            P.do("vector", lambda e, cc=cc: e.tensor_tensor(
                src(cc), src(cc), self.mrep[:], ALU.subtract),
                deps=[tmr])
        for cc in range(CC):
            tm_ = P.do("vector", lambda e, cc=cc: e.tensor_tensor(
                src(cc), src(cc), self.arep[:], ALU.mult),
                deps=[tar], inc="dve")
            ta_ = P.do("vector", lambda e, cc=cc: e.tensor_tensor(
                self.x_sb[:, cc, :], self.x_sb[:, cc, :], src(cc), ALU.add),
                deps=[tm_], inc="dve")
            toks.append(ta_)
            if mirror:
                tq = P.do("scalar", lambda e, cc=cc: e.activation(
                    self.a2[:, cc, :], self.x_sb[:, cc, :], AF.Square),
                    deps=[ta_, tm_], inc="act")
                msq.append([tq])
        for cc in warm_toks:
            self.warm([toks[cc]])
        return toks[-1], toks, msq

    def lnd(self, msq):
        P, ps = self.P, self.ps
        t_sq = None
        for cc in range(CC):
            for j in range(4):
                t_sq = P.do("tensor", lambda e, cc=cc, j=j: e.matmul(
                    ps[0:1, 2048 + j * 512:2048 + (j + 1) * 512], self.onesm[:],
                    self.a2[:, cc, j * 512:(j + 1) * 512],
                    start=(cc == 0), stop=(cc == CC - 1)),
                    deps=msq[cc], inc="pe" if j == 3 else None)
        t3 = P.do("vector", lambda e: e.tensor_tensor(
            self.rows[0:1, :], self.mrep2[0:1, :], self.mrep2[0:1, :],
            ALU.mult), inc="dve")
        t4 = P.do("vector", lambda e: e.tensor_tensor(
            self.rows[0:1, :], ps[0:1, 2048:4096], self.rows[0:1, :],
            ALU.subtract), deps=[t3, t_sq], inc="dve")
        t5 = P.do("scalar", lambda e: e.activation(
            self.rows[0:1, :], self.rows[0:1, :], AF.Ln, bias=EPS, scale=1.0),
            deps=[t4], inc="act")
        t6 = P.do("scalar", lambda e: e.activation(
            self.arep[0:1, :], self.rows[0:1, :], AF.Exp, scale=-0.5),
            deps=[t5], inc="act")
        self.warm([t4])
        self.warm([t5])
        ta = None
        for j in range(4):
            ta = P.do("tensor", lambda e, j=j: e.matmul(
                ps[0:128, 2048 + j * 512:2048 + (j + 1) * 512],
                self.ones128[0:1, :],
                self.arep[0:1, j * 512:(j + 1) * 512], start=True, stop=True),
                deps=[t6], inc="pe" if j == 3 else None)
        tar = P.do("scalar", lambda e: e.activation(
            self.arep[:, :], ps[:, 2048:4096], AF.Copy), deps=[ta], inc="act")
        xln_cc = []
        for cc in range(CC):
            P.do("vector", lambda e, cc=cc: e.tensor_tensor(
                self.z(cc), self.x_sb[:, cc, :], self.mrep2[:], ALU.subtract),
                deps=[t_sq])
        for cc in range(CC):
            xln_cc.append(P.do("vector", lambda e, cc=cc: e.tensor_tensor(
                self.z(cc), self.z(cc), self.arep[:], ALU.mult),
                deps=[tar], inc="dve"))
        return xln_cc

    def attention(self, z_cc, att_tok):
        self.z_cc = z_cc
        self.pending_pe = None
        """Returns final att pool token (att in a2). Weights for group 0 are
        preloaded (self.wtok_qkv); each group prefetches the next group's
        weights (and for g==3 the out-proj halves) during its second half."""
        P, ps = self.P, self.ps
        self.rrg = {}
        self.slot_g = {}
        for g in range(NG):
            init = att_tok or self.b_free
            self.att_i = 0
            self.texp_reg = [init, init, init]
            self.tpv_par = [init, init]
            self.out_guard = [init]
            twq, twk, twv = self.wtok_qkv
            tvev = None
            for ocg in range(2):
                if self.pending_pe is not None:
                    self.pending_pe()
                    self.pending_pe = None
                tes = self.qk_mms_evacs(g, ocg, z_cc, att_tok, twq, twk)
                j0 = 2 * ocg
                tb3 = self.emit_B(g, j0, [att_tok, self.b_free,
                                          tes[2], tes[3]])
                if ocg == 0:
                    tvev = self.v_phase(g, twv, tes, 0, 8)
                tn, pr = self.norms_part("q", ocg, tes, None, None)
                if ocg == 0:
                    tvev = self.v_phase(g, twv, tes, 8, TB)
                tn, pr = self.norms_part("k", ocg, tes, pr, tn)
                pair_done = self.negm_pair(g, ocg, pr)
                if ocg == 1:
                    if g < NG - 1:
                        self.wtok_qkv = self.load_qkv_group(
                            g + 1, [pair_done])
                    else:
                        self.wtok_out = (
                            self.load_half(self.wout, 0, self.a7,
                                           [pair_done]),
                            self.load_half(self.wout, 1, self.wv_g, ()))
                att_tok, tb3n = self.head(
                    g, j0, tb3, att_tok, tvev, pair_done, red_done=pr,
                    tail=(g, j0 + 1))
                att_tok, _ = self.head(g, j0 + 1, tb3n, att_tok, tvev,
                                       pair_done, tail=None)
        if self.pending_pe is not None:
            self.pending_pe()
            self.pending_pe = None
        return att_tok

    def qk_mms_evacs(self, g, ocg, z_cc, att_tok, twq, twk):
        P, ps = self.P, self.ps
        tq = tk = None
        for cc in range(CC):
            for j in range(4):
                tq = P.do("tensor", lambda e, cc=cc, j=j, ocg=ocg: e.matmul(
                    ps[0:128, j * 512:(j + 1) * 512],
                    self.a7[:, cc, 128 * ocg:128 * (ocg + 1)],
                    self.z(cc)[:, j * 512:(j + 1) * 512],
                    start=(cc == 0), stop=(cc == CC - 1)),
                    deps=[twq, z_cc[cc], att_tok],
                    inc="pe" if j == 3 else None)
        for cc in range(CC):
            for j in range(4):
                tk = P.do("tensor", lambda e, cc=cc, j=j, ocg=ocg: e.matmul(
                    ps[0:128, 2048 + j * 512:2048 + (j + 1) * 512],
                    self.a7[:, cc, 256 + 128 * ocg:256 + 128 * (ocg + 1)],
                    self.z(cc)[:, j * 512:(j + 1) * 512],
                    start=(cc == 0), stop=(cc == CC - 1)),
                    deps=[twk], inc="pe" if j == 3 else None)
        oq = 2 * g + ocg
        ok = 8 + oq
        te0 = P.do("vector", lambda e, oq=oq: e.tensor_scalar(
            self.qs[0][0:64, :], ps[0:64, 0:2048],
            self.bqkv_sb[0:64, oq:oq + 1], 0.125, ALU.add, ALU.mult),
            deps=[tq], inc="dve")
        te1 = P.do("vector", lambda e, oq=oq: e.tensor_scalar(
            self.qs[1][64:128, :], ps[64:128, 0:2048],
            self.bqkv_sb[64:128, oq:oq + 1], 0.125, ALU.add, ALU.mult),
            deps=[tq], inc="dve")
        te2 = P.do("scalar", lambda e, ok=ok: e.activation(
            self.ks[0][0:64, :], ps[0:64, 2048:4096], AF.Identity,
            bias=self.bqkv_sb[0:64, ok:ok + 1]), deps=[tk], inc="act")
        te3 = P.do("scalar", lambda e, ok=ok: e.activation(
            self.ks[1][64:128, :], ps[64:128, 2048:4096], AF.Identity,
            bias=self.bqkv_sb[64:128, ok:ok + 1]), deps=[tk], inc="act")
        return (te0, te1, te2, te3)

    def norms_part(self, which, ocg, tes, prev_red, tn_prev):
        P, ps = self.P, self.ps
        te0, te1, te2, te3 = tes
        if which == "q":
            col, slots, toks = 0, (self.qs[0], self.qs[1]), (te0, te1)
        else:
            col, slots, toks = 4, (self.ks[0], self.ks[1]), (te2, te3)
        tq0 = P.do("scalar", lambda e, sl=slots[0]: e.activation(
            self.nsq[0:64, :], sl[0:64, :], AF.Square),
            deps=[toks[0], tn_prev], inc="act")
        tq1 = P.do("scalar", lambda e, sl=slots[1]: e.activation(
            self.nsq[64:128, :], sl[64:128, :], AF.Square),
            deps=[toks[1]], inc="act")
        self.warm([toks[0]])
        self.warm([tq0])
        tn = None
        for j in range(4):
            tn = P.do("tensor", lambda e, j=j: e.matmul(
                ps[0:2, j * 512:(j + 1) * 512], self.nsel[:, 0:2],
                self.nsq[:, j * 512:(j + 1) * 512], start=True, stop=True),
                deps=[tq0, tq1, prev_red, te0, te1],
                inc="pe" if j == 3 else None)
        prev_red = P.do("vector", lambda e, col=col, ocg=ocg: e.reduce_max(
            self.mvals[0:2, col + ocg:col + ocg + 1],
            ps[0:2, 0:2048], AX), deps=[tn], inc="dve")
        return tn, prev_red

    def negm_pair(self, g, ocg, prev_red):
        P, ps = self.P, self.ps
        c = 2 * g + ocg
        j0 = 2 * ocg
        t1 = P.do("vector", lambda e, ocg=ocg, c=c: e.scalar_tensor_tensor(
            self.smf[0:2, 1:2], self.mvals[0:2, 4 + ocg:5 + ocg],
            -1.0 / 16.0, self.bbp[0:2, c:c + 1], ALU.mult, ALU.subtract),
            deps=[prev_red], inc="dve", strict=True)
        t4 = P.do("vector", lambda e, ocg=ocg: e.scalar_tensor_tensor(
            self.negmb2[0:2, 0:1], self.mvals[0:2, ocg:ocg + 1],
            -4.0, self.smf[0:2, 1:2], ALU.mult, ALU.add),
            deps=[t1], inc="dve", strict=True)
        t4c = P.do("vector", lambda e: e.tensor_scalar_mul(
            self.negval[0:2, :], self.onesw[0:2, :], self.negmb2[0:2, 0:1]),
            deps=[t4], inc="dve", strict=True)
        t5 = P.do("tensor", lambda e: e.matmul(
            ps[0:128, 3584:3586], self.negval[0:2, :],
            self.ident2[0:2, 0:2], start=True, stop=True),
            deps=[t4c], inc="pe")
        last = P.do("vector", lambda e, j0=j0: e.tensor_copy(
            self.negm[:, j0:j0 + 2], ps[0:128, 3584:3586]),
            deps=[t5], inc="dve")
        return last

    def v_phase(self, g, twv, tes, tb0, tb1):
        P, ps = self.P, self.ps
        te0, te1, te2, te3 = tes
        if tb0 == 0:
            for v in self.vt:
                P.do("vector", lambda e, v=v: e.memset(v[:, :, 64:65], 1.0),
                     deps=[te2, te3])
                P.do("vector", lambda e, v=v: e.memset(v[:, :, 97:98], 1.0))
                P.do("vector", lambda e, v=v: e.memset(v[:, :, 65:97], 0.0))
                P.do("vector", lambda e, v=v: e.memset(v[:, :, 98:129], 0.0))
        tvev = None
        for tb in range(tb0, tb1):
            reg = 2048 + (tb % 2) * 512
            tvz = None
            for cc in range(CC):
                tvz = P.do("tensor", lambda e, tb=tb, cc=cc, reg=reg: e.matmul(
                    ps[0:128, reg:reg + 256], self.zblk(cc, tb),
                    self.wv_g[:, cc, 0:256],
                    start=(cc == 0), stop=(cc == CC - 1)),
                    deps=[twv, te2, te3, self.z_cc[cc],
                          self._vwar.get(tb % 2)],
                    inc="pe" if cc == CC - 1 else None)
            for j in range(4):
                dst = self.vt[j // 2]
                off = 0 if j % 2 == 0 else 129
                tok = P.do("vector",
                    lambda e, tb=tb, j=j, reg=reg, dst=dst, off=off:
                    e.tensor_copy(dst[:, tb, off:off + 64],
                                  ps[:, reg + 64 * j:reg + 64 * (j + 1)]),
                    deps=[tvz], inc="dve" if j == 3 else None)
            self._vwar[tb % 2] = tok
            tvev = tok
        return tvev

    def emit_B(self, g, j, deps, col0=3072):
        P, ps = self.P, self.ps
        h = 4 * g + j
        par = j % 2
        ksl = self.ks[par]
        tm1 = P.do("vector", lambda e, h=h, par=par: e.tensor_scalar(
            self.mh_t[:, 64 * (1 - par):64 * (1 - par) + 64], self.dm[:],
            self.grel[:, h:h + 1], None, ALU.mult),
            deps=list(deps), inc="dve")
        rlo = 64 if par == 0 else 0
        tb3 = tm1
        for half in range(2):
            tbm = None
            for hh in range(2):
                tbm = P.do("tensor", lambda e, half=half, hh=hh, col0=col0:
                    e.matmul(
                        ps[0:128, col0 + hh * 512:col0 + (hh + 1) * 512],
                        self.mh_t[:],
                        self.cs[:, 1024 * half + 512 * hh:
                                1024 * half + 512 * (hh + 1)],
                        start=True, stop=True),
                    deps=[tb3], inc="pe" if hh == 1 else None)
            tb3 = P.do("vector", lambda e, half=half, rlo=rlo, ksl=ksl,
                              col0=col0: e.tensor_copy(
                    ksl[rlo:rlo + 64, 1024 * half:1024 * (half + 1)],
                    ps[rlo:rlo + 64, col0:col0 + 1024]),
                deps=[tbm], inc="dve")
        self.b_free = tb3
        return tb3

    def head(self, g, j, tb3, att_prev, tvev, pair_done, red_done=None,
             tail=None):
        P, ps = self.P, self.ps
        h = 4 * g + j
        par = j % 2
        qsl, ksl = self.qs[par], self.ks[par]
        R = (0, 1024, 3072)
        vview = self.vt[j // 2]
        lc = (0, 65) if par == 0 else (65, 193)
        drow = 64 if par == 0 else 32
        a, b = (0, 64) if par == 0 else (64, 128)
        att_tok = att_prev
        tb3n = None

        def gate(i, kb, qh):
            if red_done is None or qh != 0 or kb >= 3:
                return []
            return [pair_done] if (i % 3 == 2) else [red_done]

        def emit_S(i, qh, kb, extra):
            tmm = None
            for jj in range(2):
                tmm = P.do("tensor",
                    lambda e, i=i, kb=kb, jj=jj, qh=qh, qsl=qsl, ksl=ksl:
                    e.matmul(
                        ps[0:128, R[i % 3] + jj * 512:
                           R[i % 3] + (jj + 1) * 512],
                        ksl[:, 128 * kb:128 * (kb + 1)],
                        qsl[:, 1024 * qh + 512 * jj:
                            1024 * qh + 512 * (jj + 1)],
                        start=True, stop=True),
                    deps=[tb3, self.texp_reg[i % 3]] + extra,
                    inc="pe" if jj == 1 else None)
            return tmm

        for qh in range(2):
            i0 = self.att_i
            first = [self.u_tok] if (qh == 0) else []
            tmm_k = {}
            tmm_k[0] = emit_S(i0, qh, 0, first + gate(i0, 0, qh))
            tmm_k[1] = emit_S(i0 + 1, qh, 1, gate(i0 + 1, 1, qh))
            if self.pending_pe is not None:
                self.pending_pe()
                self.pending_pe = None
            tpv = None
            for kb in range(TB):
                i = i0 + kb
                pr3 = i % 3
                pr2 = i % 2
                if kb + 2 < TB:
                    tmm_k[kb + 2] = emit_S(i + 2, qh, kb + 2,
                                           gate(i + 2, kb + 2, qh))
                texp = P.do("scalar",
                    lambda e, pr2=pr2, st0=R[pr3], j=j: e.activation(
                        self.pbuf[pr2][:], ps[:, st0:st0 + 1024], AF.Exp,
                        bias=self.negm[:, j:j + 1]),
                    deps=[tmm_k[kb], self.tpv_par[pr2]]
                         + ([pair_done] if (kb == 0 and qh == 0) else []),
                    inc="act")
                self.texp_reg[pr3] = texp
                for jj in range(2):
                    tpv = P.do("tensor",
                        lambda e, kb=kb, jj=jj, pr2=pr2, lc=lc, vview=vview:
                        e.matmul(
                            ps[0:lc[1] - lc[0],
                               2048 + jj * 512:2048 + (jj + 1) * 512],
                            vview[:, kb, lc[0]:lc[1]],
                            self.pbuf[pr2][:, jj * 512:(jj + 1) * 512],
                            start=(kb == 0), stop=(kb == TB - 1)),
                        deps=[texp] + (([tvev] + self.out_guard)
                                       if kb == 0 else []),
                        inc="pe" if jj == 1 else None)
                self.tpv_par[pr2] = tpv
            self.att_i = i0 + TB
            t1 = P.do("scalar", lambda e, drow=drow: e.activation(
                self.rows[drow:drow + 1, 0:1024],
                ps[drow:drow + 1, 2048:3072], AF.Ln), deps=[tpv], inc="act")
            t2 = P.do("scalar", lambda e, drow=drow: e.activation(
                self.rrow[drow:drow + 1, :],
                self.rows[drow:drow + 1, 0:1024], AF.Exp, scale=-1.0),
                deps=[t1, self.rrg.get(drow)], inc="act")
            rreg = (self.att_i + 2) % 3
            reg_guard = self.texp_reg[rreg]
            rv = P.cnt["rcp"]
            P.cnt["rcp"] = rv + 4
            slot = self.mrep[a:b, 1024 * qh:1024 * (qh + 1)]
            sg = self.slot_g.get((par, qh))
            attv = self.a2[a:b, h // 2, 1024 * qh:1024 * (qh + 1)]
            tos = P.do("vector", lambda e, attv=attv, a=a, b=b: e.tensor_copy(
                attv, ps[a:b, 2048:3072]), deps=[tpv], inc="dve")
            self.out_guard = [tos, t1]
            if qh == 1 and tail is not None:
                tb3n = self.emit_B(tail[0], tail[1],
                                   [self.b_free, tos, t1], col0=2048)

            def emit_recip(t2=t2, drow=drow, c0=R[rreg], rg=reg_guard,
                           a=a, b=b, h=h, slot=slot, sg=sg, rv=rv,
                           attv=attv, tos=tos):
                for jj in range(2):
                    P.do("tensor", lambda e, jj=jj, drow=drow, c0=c0:
                        e.matmul(
                            ps[0:128, c0 + jj * 512:c0 + (jj + 1) * 512],
                            self.ones128[drow:drow + 1, :],
                            self.rrow[drow:drow + 1,
                                      jj * 512:(jj + 1) * 512],
                            start=True, stop=True),
                        deps=[t2, rg],
                        inc="rcp" if jj == 1 else None, raw_inc=True)
                self.rrg[drow] = ("rcp", rv + 1)
                P.do("vector",
                    lambda e, a=a, b=b, c0=c0, slot=slot:
                    e.tensor_copy(slot, ps[a:b, c0:c0 + 1024]),
                    deps=[("rcp", rv + 1), sg], inc="rcp", raw_inc=True)
                self.texp_reg_set(c0, ("rcp", rv + 2))
                P.do("vector", lambda e, attv=attv, slot=slot:
                    e.tensor_tensor(attv, attv, slot, ALU.mult),
                    deps=[tos], inc="rcp", raw_inc=True)
                P.do("vector",
                    lambda e, attv=attv, a=a, b=b, h=h: e.tensor_scalar(
                        attv, attv,
                        self.bqkv_sb[a:b, 16 + h // 2:17 + h // 2],
                        None, ALU.add), inc="rcp", raw_inc=True)

            self.pending_pe = lambda: emit_recip()
            att_tok = ("rcp", rv + 4)
            self.slot_g[(par, qh)] = att_tok
        return att_tok, tb3n

    def out_proj(self, att_done):
        P, ps = self.P, self.ps
        tev = [None, None]
        pe_half = []
        sdeps, qdeps = [[] for _ in range(CC)], [[] for _ in range(CC)]
        for half in range(2):
            buf = self.a7 if half == 0 else self.wv_g
            ct = self.wtok_out[half]
            tmm = None
            for oc_ in range(4):
                oc = 4 * half + oc_
                reg = (oc_ % 2) * 2048
                for cc in range(CC):
                    for j in range(4):
                        tmm = P.do("tensor",
                            lambda e, cc=cc, j=j, oc_=oc_, reg=reg, buf=buf:
                            e.matmul(
                                ps[0:128, reg + j * 512:reg + (j + 1) * 512],
                                buf[:, cc, 128 * oc_:128 * (oc_ + 1)],
                                self.a2[:, cc, j * 512:(j + 1) * 512],
                                start=(cc == 0), stop=(cc == CC - 1)),
                            deps=[ct, att_done, tev[oc_ % 2]],
                            inc="pe" if j == 3 else None)
                tev[oc_ % 2] = P.do("vector",
                    lambda e, oc=oc, reg=reg: e.tensor_scalar(
                        self.z(oc), ps[:, reg:reg + 2048],
                        self.bout_sb[:, oc:oc + 1], None, ALU.add),
                    deps=[tmm], inc="dve")
                tsq = P.do("scalar", lambda e, oc=oc: e.activation(
                    self.a4c(oc), self.z(oc), AF.Square),
                    deps=[tev[oc_ % 2]], inc="act")
                sdeps[oc] = [tev[oc_ % 2]]
                qdeps[oc] = [tsq]
            pe_half.append(tmm)
        self.pe_half = pe_half
        sdeps[0] = sdeps[0] + [tev[0], tev[1]]
        qdeps[0] = qdeps[0] + [tev[0], tev[1]]
        return sdeps, qdeps

    def ffn(self, xln_cc):
        P, ps = self.P, self.ps
        xln = self.z
        xprev = lambda cc: self.a1[:, cc, 0:T]
        tk_cc = []
        for cc in range(CC):
            P.do("vector", lambda e, cc=cc: e.tensor_tensor(
                self.a2[:, cc, :], xln(cc), xprev(cc), ALU.subtract),
                deps=[xln_cc[cc]])
            tk_cc.append([P.do("vector", lambda e, cc=cc:
                e.scalar_tensor_tensor(
                    self.a2[:, cc, :], self.a2[:, cc, :],
                    self.muk_sb[:, cc:cc + 1], xprev(cc), ALU.mult, ALU.add),
                inc="dve")])

        ka_cc = [[] for _ in range(CC)]

        def relu_sq(dch, reg, tmm):
            t1 = P.do("vector", lambda e, dch=dch, reg=reg: e.tensor_scalar(
                self.ka_v[:, dch, :], ps[:, reg:reg + 2048], 0.0, None,
                ALU.max), deps=[tmm], inc="dve")
            tq = P.do("scalar", lambda e, dch=dch: e.activation(
                self.ka_v[:, dch, :], self.ka_v[:, dch, :], AF.Square),
                deps=[t1], inc="act")
            ka_cc[dch] = [tq]
            return tq

        rk = self.matmul_dd(self.wtok_wk, lambda cc: self.a2[:, cc, :], [],
                            relu_sq, rhs_deps_cc=tk_cc)
        self.wtok_wr = (self.load_half(self.wrt, 0, self.a7,
                                       [rk["pe_half"][0]]),
                        self.load_half(self.wrt, 1, self.wv_g,
                                       [rk["pe_half"][1]]))
        tr_cc = []
        for cc in range(CC):
            P.do("vector", lambda e, cc=cc: e.tensor_tensor(
                self.a2[:, cc, :], xln(cc), xprev(cc), ALU.subtract),
                deps=[xln_cc[cc], rk["mm_done"]])
            tr_cc.append([P.do("vector", lambda e, cc=cc:
                e.scalar_tensor_tensor(
                    self.a2[:, cc, :], self.a2[:, cc, :],
                    self.mur_sb[:, cc:cc + 1], xprev(cc), ALU.mult, ALU.add),
                inc="dve")])

        def sig(dch, reg, tmm):
            return P.do("scalar", lambda e, dch=dch, reg=reg: e.activation(
                self.a1[:, dch, 1:T + 1], ps[:, reg:reg + 2048], AF.Sigmoid),
                deps=[tmm], inc="act")

        rr = self.matmul_dd(self.wtok_wr, lambda cc: self.a2[:, cc, :], [],
                            sig, rhs_deps_cc=tr_cc)
        self.wtok_wv = (self.load_half(self.wvt, 0, self.a7,
                                       [rr["pe_half"][0]]),
                        self.load_half(self.wvt, 1, self.wv_g,
                                       [rr["pe_half"][1]]))
        sdeps, qdeps = [[] for _ in range(CC)], [[] for _ in range(CC)]

        def fmul(dch, reg, tmm):
            tf = P.do("vector", lambda e, dch=dch, reg=reg: e.tensor_tensor(
                self.a2[:, dch, :], ps[:, reg:reg + 2048],
                self.a1[:, dch, 1:T + 1], ALU.mult), deps=[tmm], inc="dve")
            tq = P.do("scalar", lambda e, dch=dch: e.activation(
                self.a1[:, dch, 1:T + 1], self.a2[:, dch, :], AF.Square),
                deps=[tf], inc="act")
            sdeps[dch] = [tf]
            qdeps[dch] = [tq]
            return tq

        rv = self.matmul_dd(self.wtok_wv, lambda cc: self.ka_v[:, cc, :],
                            [rr["evac_done"]], fmul, rhs_deps_cc=ka_cc)
        self.pe_half = rv["pe_half"]
        sdeps[0] = sdeps[0] + [sdeps[6][0], sdeps[7][0]]
        qdeps[0] = qdeps[0] + [sdeps[6][0], sdeps[7][0]]
        return sdeps, qdeps

    def matmul_dd(self, wtoks, rhs, rhs_deps, evac, rhs_deps_cc=None):
        P, ps = self.P, self.ps
        tevs = [None, None]
        mm_done = None
        ev_last = None
        pe_half = []
        for half in range(2):
            buf = self.a7 if half == 0 else self.wv_g
            ct = wtoks[half]
            tmm = None
            for d_ in range(4):
                dch = 4 * half + d_
                reg = (d_ % 2) * 2048
                for cc in range(CC):
                    dcc = rhs_deps_cc[cc] if rhs_deps_cc else []
                    for j in range(4):
                        tmm = P.do("tensor",
                            lambda e, cc=cc, j=j, d_=d_, reg=reg, buf=buf:
                            e.matmul(
                                ps[0:128, reg + j * 512:reg + (j + 1) * 512],
                                buf[:, cc, 128 * d_:128 * (d_ + 1)],
                                rhs(cc)[:, j * 512:(j + 1) * 512],
                                start=(cc == 0), stop=(cc == CC - 1)),
                            deps=list(rhs_deps) + dcc + [ct, tevs[d_ % 2]],
                            inc="pe" if j == 3 else None)
                mm_done = tmm
                tevs[d_ % 2] = evac(dch, reg, tmm)
                ev_last = tevs[d_ % 2]
            pe_half.append(tmm)
        return {"mm_done": mm_done, "evac_done": ev_last, "pe_half": pe_half}

    def build(self):
        P = self.P
        x_toks = []
        for i in range(4):
            x_toks.append(P.do("sync", lambda e, i=i: e.dma_start(
                out=self.x_sb[:, 2 * i:2 * i + 2, :],
                in_=self.x_hbm[0].rearrange(
                    "(c p) t -> p c t", p=128)[:, 2 * i:2 * i + 2, :]),
                inc="xi%d" % i, n=16))
        self.prelude()
        self._vwar = {}
        self.b_free = self.P.a5g
        self.u_tok = None
        store_toks = None
        att_tok = None
        self.wtok_qkv = self.load_qkv_group(0, [])
        for s in range(NSEQ):
            if s > 0:
                x_toks = []
                for i in range(4):
                    x_toks.append(P.do("sync", lambda e, s=s, i=i: e.dma_start(
                        out=self.x_sb[:, 2 * i:2 * i + 2, :],
                        in_=self.x_hbm[s].rearrange(
                            "(c p) t -> p c t", p=128)[:, 2 * i:2 * i + 2, :]),
                        deps=[store_toks[i]],
                        inc="xi%d" % i, n=16))
                self.warm([store_toks[0]])
            self.cur_seq = s
            z_cc = self.site1(x_toks)
            if s == 0:
                self.dump("z", self.a1[:], z_cc[-1])
            att_tok = self.attention(z_cc, att_tok)
            if s == 0:
                self.dump("att", self.a2[:], att_tok)
            sd_c, qd_c = self.out_proj(att_tok)
            self.wtok_wk = (self.load_half(self.wkt, 0, self.a7,
                                           [self.pe_half[0]]),
                            self.load_half(self.wkt, 1, self.wv_g,
                                           [self.pe_half[1]]))
            t_x1, _, msq = self.ln_residual(
                self.z, sd_c, self.a4c, qd_c, mirror=True)
            if s == 0:
                self.dump("x1", self.x_sb[:], t_x1)
            xln_cc = self.lnd(msq)
            if s == 0:
                self.dump("xln", self.a1[:], xln_cc[-1])
            sd_e, qd_e = self.ffn(xln_cc)
            if s + 1 < NSEQ:
                self.wtok_qkv = self.load_qkv_group(
                    0, [self.pe_half[0], self.pe_half[1]])
            t_x2, toks, _ = self.ln_residual(
                lambda cc: self.a2[:, cc, :], sd_e,
                lambda cc: self.a1[:, cc, 1:T + 1], qd_e, mirror=False,
                warm_toks=(0, 1))

            st_toks = []
            for i in range(4):
                st_toks.append(P.do("gpsimd", lambda e, s=s, i=i: e.dma_start(
                    out=self.out_hbm[s].rearrange(
                        "(c p) t -> p c t", p=128)[:, 2 * i:2 * i + 2, :],
                    in_=self.x_sb[:, 2 * i:2 * i + 2, :]),
                    deps=[toks[2 * i], toks[2 * i + 1]],
                    inc="so%d" % i, n=16))
            store_toks = st_toks

        for i in range(4):
            fv = P.cnt["so%d" % i]
            P.q["gpsimd"].append(
                lambda e, v=fv, i=i: e.wait_ge(P.sems["so%d" % i], v))
        if P.cnt["out"]:
            fv = P.cnt["out"]
            P.q["gpsimd"].append(lambda e, v=fv: e.wait_ge(P.sems["out"], v))


def build_program():
    from contextlib import ExitStack
    nc = bass.Bass("TRN2", target_bir_lowering=False)
    P = Prog()
    with ExitStack() as stack:
        k = K(nc, P, stack)
        k.build()
        with nc.Block() as block:
            @block.sync
            def _(e):
                for f in P.q["sync"]:
                    f(e)

            @block.tensor
            def _(e):
                for f in P.q["tensor"]:
                    f(e)

            @block.scalar
            def _(e):
                for f in P.q["scalar"]:
                    f(e)

            @block.vector
            def _(e):
                for f in P.q["vector"]:
                    f(e)

            @block.gpsimd
            def _(e):
                for f in P.q["gpsimd"]:
                    f(e)
    return nc


def repack_w(w):
    w = np.asarray(w, np.float32)
    n = w.shape[1] // 128
    t = w.reshape(CC, 128, n, 128)
    return np.ascontiguousarray(t.transpose(2, 1, 0, 3)).reshape(n, 128, 1024)


def prep_in_maps(inputs):
    x = np.asarray(inputs["L0_attn_x"])
    wqkv = repack_w(inputs["L0_attn_w_qkv"])
    wout = repack_w(inputs["L0_attn_w_out"])
    wrt = repack_w(np.asarray(inputs["L0_ffn_W_r"], np.float32).T)
    wkt = repack_w(np.asarray(inputs["L0_ffn_W_k"], np.float32).T)
    wvt = repack_w(np.asarray(inputs["L0_ffn_W_v"], np.float32).T)
    col = lambda v, n: np.ascontiguousarray(
        np.asarray(v, np.float32).reshape(n, 128).T)
    bqkv = col(inputs["L0_attn_b_qkv"], 24)
    bout = col(inputs["L0_attn_b_out"], 8)
    gln = col(inputs["L0_attn_ln_g"], 8)
    bln = col(inputs["L0_attn_ln_b"], 8)
    mur = col(inputs["L0_ffn_mu_r"], 8)
    muk = col(inputs["L0_ffn_mu_k"], 8)
    w_rel = np.asarray(inputs["L0_attn_w_rel"], np.float32)
    wsw = np.ascontiguousarray(
        w_rel.reshape(32, 2, 16)[:, ::-1, :].reshape(64, 16))
    wrel2 = np.ascontiguousarray(np.concatenate([w_rel, wsw], axis=0))
    u2, trig, dmask, sgn = host_constants()

    shared = {"wqkv": wqkv, "wout": wout, "wrt": wrt, "wkt": wkt, "wvt": wvt,
              "bqkv": bqkv, "bout": bout, "gln": gln, "bln": bln,
              "mur": mur, "muk": muk, "wrel2": wrel2, "sgn": sgn,
              "u2": u2, "trig": trig, "dmask": dmask}
    in_maps = []
    for core in range(8):
        xs = np.ascontiguousarray(
            x[2 * core:2 * core + 2].transpose(0, 2, 1)).astype(np.float32)
        m = dict(shared)
        m["x"] = xs
        in_maps.append(m)
    return in_maps


def kernel(**inputs):
    in_maps = prep_in_maps(inputs)
    nc = build_program()
    res = run_bass_kernel_spmd(nc, in_maps, core_ids=list(range(8)))
    outs = [np.asarray(res.results[c]["out"]).transpose(0, 2, 1)
            for c in range(8)]
    return np.concatenate(outs, axis=0).astype(np.float32)
